# version 02
# baseline (speedup 1.0000x reference)
import numpy as np
from contextlib import ExitStack

import concourse.bass as bass
import concourse.mybir as mybir
from concourse.bass_utils import run_bass_kernel_spmd

F32 = mybir.dt.float32
BF16 = mybir.dt.bfloat16
AF = mybir.ActivationFunctionType
ALU = mybir.AluOpType
AXX = mybir.AxisListType.X

P = 128
D = 1024
TOK = 2048
NT = TOK // P
NCORES = 8
EPS_LN = 1e-5
EPS_DEN = 1e-6

ENGS = ("sync", "vector", "scalar", "tensor", "gpsimd")

C_EPS_LN = 0
C_ZERO = 1
C_EPS_DEN = 2
C_ONE = 3


class Sched:

    def __init__(self, nc, es):
        self.nc = nc
        self.es = es
        self.ops = {e: [] for e in ENGS}
        self._sems = {}
        self._cnt = {}

    def sem(self, name):
        if name not in self._sems:
            self._sems[name] = self.es.enter_context(self.nc.semaphore(name))
            self._cnt[name] = 0
        return name

    def op(self, eng, fn, sig=None, k=1):
        if sig is not None:
            self.sem(sig)
            sem = self._sems[sig]
            self._cnt[sig] += k

            def wrapped(e, fn=fn, sem=sem, k=k):
                fn(e).then_inc(sem, k)
            self.ops[eng].append(wrapped)
        else:
            self.ops[eng].append(fn)

    def dma(self, eng, out, in_, sig=None):
        self.op(eng, lambda e, o=out, i=in_: e.dma_start(out=o, in_=i), sig=sig, k=16)

    def wait(self, eng, sig, cnt=None):
        self.sem(sig)
        sem = self._sems[sig]
        want = self._cnt[sig] if cnt is None else cnt
        self.ops[eng].append(lambda e, sem=sem, want=want: e.wait_ge(sem, want))

    def opc(self, eng, fn):
        hz = "hz_" + eng
        self.wait(eng, hz)
        self.op(eng, fn, sig=hz)

    def val(self, sig):
        self.sem(sig)
        return self._cnt[sig]

    def build(self):
        with self.nc.Block() as block:
            for eng in ENGS:
                ops = self.ops[eng]
                if not ops:
                    continue

                def body(e, ops=ops):
                    for f in ops:
                        f(e)
                getattr(block, eng)(body)


class Prog:

    def __init__(self, name):
        self.nc = bass.Bass("TRN2", target_bir_lowering=False, name=name,
                            detect_race_conditions=False)
        self.es = ExitStack()
        self.s = Sched(self.nc, self.es)
        self._recip_names = []

    def inp(self, name, shape, dtype=F32):
        return self.nc.dram_tensor(name, list(shape), dtype, kind="ExternalInput").ap()

    def out(self, name, shape, dtype=F32):
        return self.nc.dram_tensor(name, list(shape), dtype, kind="ExternalOutput").ap()

    def sb(self, name, shape, dtype=F32):
        return self.es.enter_context(self.nc.sbuf_tensor(name, list(shape), dtype))

    def ps(self, name, shape, dtype=F32):
        return self.es.enter_context(self.nc.psum_tensor(name, list(shape), dtype))

    def act_recip(self, scalar, out, in_):
        i = scalar.activation(out, in_, AF.Sqrt)
        self._recip_names.append(i.ins.name)
        return i

    def finish(self):
        self.s.build()
        self.es.close()
        if self._recip_names:
            want = set(self._recip_names)
            n = 0
            for b in self.nc.main_func.blocks:
                insts = b.instructions
                for k, ins in enumerate(insts):
                    if type(ins).__name__ == "InstActivation" and ins.name in want:
                        insts[k] = ins.__replace__(func=AF.Reciprocal)
                        n += 1
            assert n == len(want), (n, len(want))
        return self.nc



def host_consts():
    c = {}
    c["ident_bf"] = np.eye(P, dtype=np.float32)
    c["triu_bf"] = np.triu(np.ones((P, P), np.float32))
    c["ones_row"] = np.ones((1, P), np.float32)
    c["ones_col"] = np.ones((P, 1), np.float32)
    c["ones_512"] = np.ones((1, 512), np.float32)
    c["ladd24"] = np.broadcast_to(np.arange(1, 25, dtype=np.float32), (P, 24)).copy()
    c["mod6"] = np.broadcast_to(np.array([8, 16, 24, 32, 40, 48], np.float32), (P, 6)).copy()
    c["eq9"] = np.broadcast_to(np.arange(9, dtype=np.float32), (P, 9)).copy()
    consts = np.zeros((P, 8), np.float32)
    consts[:, C_EPS_LN] = EPS_LN
    consts[:, C_ZERO] = 0.0
    consts[:, C_EPS_DEN] = EPS_DEN
    consts[:, C_ONE] = 1.0
    c["consts"] = consts
    return c


def to_bf16(a):
    import ml_dtypes
    return np.ascontiguousarray(a).astype(ml_dtypes.bfloat16)



def emit_ln_tile(p, s, R_t, consts, scr, out_bf, out_f32=None, rms_only=False,
                 extra_scale=None, sig=None):
    sq, ss, summ, mean, ms, var, std, rstd = (
        scr["sq"], scr["ss"], scr["sum"], scr["mean"], scr["ms"], scr["var"],
        scr["std"], scr["rstd"])
    s.op("scalar", lambda e: e.activation(sq[:], R_t, AF.Square), sig="ln_s2v")
    if not rms_only:
        s.opc("vector", lambda e: e.reduce_sum(summ[:], R_t, axis=AXX))
        s.opc("vector", lambda e: e.tensor_scalar(mean[:], summ[:], 1.0 / D,
                                                  None, ALU.mult))
        s.opc("vector", lambda e: e.tensor_mul(ms[:], mean[:], mean[:]))
    s.wait("vector", "ln_s2v")
    s.opc("vector", lambda e: e.reduce_sum(ss[:], sq[:], axis=AXX))
    s.wait("vector", "hz_vector")
    if rms_only:
        s.op("vector", lambda e: e.tensor_scalar(var[:], ss[:], 1.0 / D, None,
                                                 ALU.mult), sig="ln_v2s")
    else:
        s.op("vector", lambda e: e.scalar_tensor_tensor(var[:], ss[:], 1.0 / D,
                                                        ms[:], ALU.mult,
                                                        ALU.subtract),
             sig="ln_v2s")
    s.wait("scalar", "ln_v2s")
    s.op("scalar", lambda e: e.activation(std[:], var[:], AF.Sqrt,
                                          bias=consts[:, C_EPS_LN:C_EPS_LN + 1]),
         sig="ln_s2v")
    s.wait("vector", "ln_s2v")
    s.opc("vector", lambda e: e.reciprocal(rstd[:], std[:]))
    if extra_scale is not None:
        s.opc("vector", lambda e: e.tensor_mul(rstd[:], rstd[:], extra_scale))
    s.wait("vector", "hz_vector")
    if rms_only:
        if out_f32 is not None:
            s.op("vector", lambda e: e.tensor_scalar(out_f32, R_t, rstd[:, 0:1],
                                                     None, ALU.mult),
                 sig=(sig if out_bf is None else None))
            if out_bf is not None:
                s.op("vector", lambda e: e.tensor_copy(out_bf, out_f32), sig=sig)
        else:
            s.op("vector", lambda e: e.tensor_scalar(out_bf, R_t, rstd[:, 0:1],
                                                     None, ALU.mult), sig=sig)
    else:
        if out_f32 is not None:
            s.op("vector", lambda e: e.tensor_scalar(out_f32, R_t, mean[:, 0:1],
                                                     rstd[:, 0:1], ALU.subtract,
                                                     ALU.mult),
                 sig=(sig if out_bf is None else None))
            if out_bf is not None:
                s.op("vector", lambda e: e.tensor_copy(out_bf, out_f32), sig=sig)
        else:
            s.op("vector", lambda e: e.tensor_scalar(out_bf, R_t, mean[:, 0:1],
                                                     rstd[:, 0:1], ALU.subtract,
                                                     ALU.mult), sig=sig)


def emit_transpose8(p, s, src_bf, dst_fm, ident_bf, pt, src_cols=D,
                    ch="tr", evac=None):
    nk = src_cols // P
    t2v, v2t = ch + "_t2v", ch + "_v2t"
    s.wait("tensor", v2t)
    for k in range(nk):
        bank = pt[k % 2]
        if k >= 2:
            s.wait("tensor", v2t)
        s.op("tensor",
             lambda e, k=k, bank=bank: e.transpose(
                 bank[:], src_bf[:, k * P:(k + 1) * P], ident_bf[:]),
             sig=t2v)
        s.wait("vector", t2v)
        if evac is None:
            s.op("vector",
                 lambda e, k=k, bank=bank: e.tensor_copy(
                     dst_fm[:, k * P:(k + 1) * P], bank[:]),
                 sig=v2t)
        else:
            s.op("vector",
                 lambda e, k=k, bank=bank: evac(e, k, bank),
                 sig=v2t)


def emit_proj_half(p, s, lnfm, w_bf, bias_bf, ones_row_bf, psum, k_cols, half,
                   nk=8, sig="mm_t2x"):
    for k in range(nk):
        s.op("tensor",
             lambda e, k=k: e.matmul(
                 psum[:], lnfm[:, k * P:(k + 1) * P],
                 w_bf[:, k * k_cols + half * 512: k * k_cols + half * 512 + 512],
                 start=(k == 0), stop=False))
    s.op("tensor",
         lambda e: e.matmul(psum[:], ones_row_bf[0:1, :],
                            bias_bf[0:1, half * 512: half * 512 + 512],
                            start=False, stop=True),
         sig=sig)


def emit_phi_evac(p, s, psum, tmp1, tmp2, out_bf, out_f32=None):
    s.wait("scalar", "phi_v2s")
    s.op("scalar", lambda e: e.activation(tmp1[:], psum[:], AF.Relu))
    s.op("scalar", lambda e: e.activation(tmp2[:], psum[:], AF.Relu, scale=-1.0))
    s.op("scalar", lambda e: e.activation(tmp2[:], tmp2[:], AF.Exp, scale=-1.0),
         sig="phi_s2v")
    s.wait("vector", "phi_s2v")
    if out_f32 is not None:
        if out_bf is None:
            s.op("vector", lambda e: e.tensor_add(out_f32, tmp1[:], tmp2[:]),
                 sig="phi_v2s")
        else:
            s.op("vector", lambda e: e.tensor_add(out_f32, tmp1[:], tmp2[:]))
            s.op("vector", lambda e: e.tensor_copy(out_bf, out_f32),
                 sig="phi_v2s")
    else:
        s.op("vector", lambda e: e.tensor_add(out_bf, tmp1[:], tmp2[:]),
             sig="phi_v2s")



def build_l0a_local():
    p = Prog("l0a_local")
    s = p.s
    x = p.inp("x", [TOK, D])
    wkT = p.inp("wkT", [D, D])
    wvT = p.inp("wvT", [D, D])
    bk = p.inp("bk", [1, D])
    bv = p.inp("bv", [1, D])
    ident = p.inp("ident_bf", [P, P], BF16)
    ones_r = p.inp("ones_row", [1, P])
    consts_i = p.inp("consts", [P, 8])

    o_kp = p.out("kp", [TOK, D], BF16)
    o_v = p.out("v", [TOK, D], BF16)
    o_sl = p.out("s_local", [D, D])
    o_ks = p.out("kp_sum", [1, D])

    R = p.sb("R", [P, NT * D])
    wstage = p.sb("wstage", [P, 8 * D])
    wk_bf = p.sb("wk_bf", [P, 8 * D], BF16)
    wv_bf = p.sb("wv_bf", [P, 8 * D], BF16)
    bias_bf = p.sb("bias_bf", [1, 2 * D], BF16)
    bias_st = p.sb("bias_st", [1, 2 * D])
    ident_bf = p.sb("identb", [P, P], BF16)
    ones_bf = p.sb("ones_bf", [1, P], BF16)
    ones_col_bf = p.sb("ones_col_bf", [P, 1], BF16)
    ones_st = p.sb("ones_st", [1, P])
    consts = p.sb("consts_s", [P, 8])
    ln_bf = p.sb("ln_bf", [P, D], BF16)
    lnfm = p.sb("lnfm", [P, 8 * P], BF16)
    scr = {k: p.sb("scr_" + k, [P, 1])
           for k in ["ss", "sum", "mean", "ms", "var", "std", "rstd"]}
    scr["sq"] = p.sb("scr_sq", [P, D])
    tmp1 = p.sb("tmp1", [P, 512])
    tmp2 = p.sb("tmp2", [P, 512])
    kp_t = [p.sb(f"kp_t{i}", [P, D], BF16) for i in range(2)]
    v_t = [p.sb(f"v_t{i}", [P, D], BF16) for i in range(2)]
    S = p.sb("S", [P, 8 * D])
    kss = p.sb("kss", [1, D])

    ptA = p.ps("ptA", [P, P], BF16)
    ptB = p.ps("ptB", [P, P], BF16)
    pp0 = p.ps("pp0", [P, 512])
    pp1 = p.ps("pp1", [P, 512])
    pcs0 = p.ps("pcs0", [1, 512])
    pcs1 = p.ps("pcs1", [1, 512])

    xr = x.rearrange("(t p) d -> p t d", p=P)
    kpr = o_kp.rearrange("(t p) d -> p t d", p=P)
    vr = o_v.rearrange("(t p) d -> p t d", p=P)

    s.dma("sync", R[:].rearrange("p (t d) -> p t d", t=NT), xr, sig="dld")
    s.dma("sync", ident_bf[:], ident, sig="dld")
    s.dma("sync", ones_st[:], ones_r, sig="dld")
    s.dma("sync", consts[:], consts_i, sig="dld")
    s.dma("sync", bias_st[0:1, 0:D], bk, sig="dld")
    s.dma("sync", bias_st[0:1, D:2 * D], bv, sig="dld")
    s.dma("sync", wstage[:].rearrange("p (k o) -> p k o", k=8),
          wkT.rearrange("(k p) o -> p k o", p=P), sig="dld")
    s.wait("vector", "dld")
    s.op("vector", lambda e: e.tensor_copy(ones_bf[:], ones_st[:]))
    s.op("vector", lambda e: e.tensor_copy(ones_col_bf[:], consts[:, C_ONE:C_ONE + 1]))
    s.op("vector", lambda e: e.tensor_copy(bias_bf[:], bias_st[:]))
    s.op("vector", lambda e: e.tensor_copy(wk_bf[:], wstage[:]), sig="cast_done")
    s.wait("sync", "cast_done")
    s.dma("sync", wstage[:].rearrange("p (k o) -> p k o", k=8),
          wvT.rearrange("(k p) o -> p k o", p=P), sig="dld2")
    s.wait("vector", "dld2")
    s.op("vector", lambda e: e.tensor_copy(wv_bf[:], wstage[:]), sig="cast_done")
    s.wait("scalar", "dld")

    for t in range(NT):
        R_t = R[:, t * D:(t + 1) * D]
        kp_c = kp_t[t % 2]
        v_c = v_t[t % 2]
        if t >= 2:
            s.wait("vector", "dst", 16 * 2 * (t - 1))
        emit_ln_tile(p, s, R_t, consts, scr, ln_bf[:], sig="ln_done")
        s.wait("tensor", "ln_done")
        emit_transpose8(p, s, ln_bf, lnfm, ident_bf, (ptA, ptB))
        s.wait("tensor", "tr_v2t")
        if t == 0:
            s.wait("tensor", "cast_done", 2)
        s.wait("tensor", "kv_evac")
        for half in range(2):
            pshalf = pp0 if half == 0 else pp1
            emit_proj_half(p, s, lnfm, wk_bf, bias_bf, ones_bf, pshalf, D, half,
                           sig="kproj")
            s.wait("scalar", "kproj")
            emit_phi_evac(p, s, pshalf, tmp1, tmp2,
                          kp_c[:, half * 512:half * 512 + 512])
        s.op("vector", lambda e, b=kp_c: e.tensor_copy(b[:, 0:1], b[:, 0:1]),
             sig="kp_done")
        s.wait("tensor", "kp_done")
        for half in range(2):
            pshalf = pp0 if half == 0 else pp1
            emit_proj_half(p, s, lnfm, wv_bf, bias_bf[0:1, D:2 * D], ones_bf,
                           pshalf, D, half, sig="vproj")
            s.wait("vector", "vproj")
            s.op("vector",
                 lambda e, ps_=pshalf, half=half, b=v_c: e.tensor_copy(
                     b[:, half * 512:half * 512 + 512], ps_[:]),
                 sig="v_done")
        s.wait("tensor", "v_done")
        for k in range(8):
            for half in range(2):
                pshalf = pp0 if half == 0 else pp1
                s.op("tensor",
                     lambda e, k=k, ps_=pshalf, half=half, kp_c=kp_c, v_c=v_c:
                     e.matmul(ps_[:], kp_c[:, k * P:(k + 1) * P],
                              v_c[:, half * 512:half * 512 + 512],
                              start=True, stop=True),
                     sig="kv_mm")
            s.wait("vector", "kv_mm")
            for half in range(2):
                pshalf = pp0 if half == 0 else pp1
                dst = S[:, k * D + half * 512: k * D + half * 512 + 512]
                if t == 0:
                    s.op("vector",
                         lambda e, d=dst, ps_=pshalf: e.tensor_copy(d, ps_[:]),
                         sig="kv_evac")
                else:
                    s.op("vector",
                         lambda e, d=dst, ps_=pshalf: e.tensor_add(d, d, ps_[:]),
                         sig="kv_evac")
            if k < 7:
                s.wait("tensor", "kv_evac")
        for half in range(2):
            pcs = pcs0 if half == 0 else pcs1
            s.op("tensor",
                 lambda e, pcs=pcs, half=half, kp_c=kp_c: e.matmul(
                     pcs[:], ones_col_bf[:, 0:1],
                     kp_c[:, half * 512:half * 512 + 512],
                     start=(t == 0), stop=(t == NT - 1)),
                 sig="cs_mm" if t == NT - 1 else None)
        s.wait("sync", "kp_done", s.val("kp_done"))
        s.wait("sync", "v_done")
        s.dma("sync", kpr[:, t, :], kp_c[:], sig="dst")
        s.dma("sync", vr[:, t, :], v_c[:], sig="dst")

    s.wait("vector", "cs_mm")
    s.op("vector", lambda e: e.tensor_copy(kss[0:1, 0:512], pcs0[:]))
    s.op("vector", lambda e: e.tensor_copy(kss[0:1, 512:1024], pcs1[:]),
         sig="kss_done")
    s.wait("sync", "kss_done")
    s.dma("sync", o_ks, kss[:], sig="dst")
    s.wait("sync", "kv_evac")
    s.dma("sync", o_sl.rearrange("(k p) e -> p k e", p=P),
          S[:].rearrange("p (k e) -> p k e", k=8), sig="dst")
    s.wait("gpsimd", "dst")

    return p.finish()



def build_l0a_finish():
    p = Prog("l0a_finish")
    s = p.s
    x = p.inp("x", [TOK, D])
    wqT = p.inp("wqT", [D, D])
    woT = p.inp("woT", [D, D])
    bq = p.inp("bq", [1, D])
    bo = p.inp("bo", [1, D])
    kp_in = p.inp("kp", [TOK, D], BF16)
    v_in = p.inp("v", [TOK, D], BF16)
    s_starts = p.inp("s_starts", [3 * D, D])
    kp_starts = p.inp("kp_starts", [1, 3 * D])
    ident = p.inp("ident_bf", [P, P], BF16)
    triu_i = p.inp("triu", [P, P])
    ones_r = p.inp("ones_row", [1, P])
    consts_i = p.inp("consts", [P, 8])

    o_r1 = p.out("r1", [TOK, D])

    R = p.sb("R", [P, NT * D])
    wstage = p.sb("wstage", [P, 4 * D])
    wq_bf = p.sb("wq_bf", [P, 8 * D], BF16)
    wo_bf = p.sb("wo_bf", [P, 8 * D], BF16)
    bias_bf = p.sb("bias_bf", [1, 2 * D], BF16)
    ident_bf = p.sb("identb", [P, P], BF16)
    triu_f = p.sb("triu_f", [P, P])
    triu_bf = p.sb("triu_b", [P, P], BF16)
    ones_bf = p.sb("ones_bf", [1, P], BF16)
    ones_col_bf = p.sb("ones_col_bf", [P, 1], BF16)
    ones_st = p.sb("ones_st", [1, P])
    consts = p.sb("consts_s", [P, 8])
    S = p.sb("S", [P, 8 * D])
    S_bf = p.sb("S_bf", [P, 8 * D], BF16)
    z_run = p.sb("z_run", [1, D])
    z_bf = p.sb("z_bf", [1, D], BF16)
    ln_bf = p.sb("ln_bf", [P, D], BF16)
    lnfm = p.sb("lnfm", [P, 8 * P], BF16)
    qp_f32 = p.sb("qp_f32", [P, D])
    qp_bf = p.sb("qp_bf", [P, D], BF16)
    qpfm = p.sb("qpfm", [P, 8 * P], BF16)
    kpfm = p.sb("kpfm", [P, 8 * P], BF16)
    kp_t = [p.sb(f"kp_t{i}", [P, D], BF16) for i in range(2)]
    v_t = [p.sb(f"v_t{i}", [P, D], BF16) for i in range(2)]
    at_bf = p.sb("at_bf", [P, P], BF16)
    o_bf = p.sb("o_bf", [P, D], BF16)
    ofm = p.sb("ofm", [P, 8 * P], BF16)
    tmp1 = p.sb("tmp1", [P, 512])
    tmp2 = p.sb("tmp2", [P, 512])
    scr = {k: p.sb("scr_" + k, [P, 1])
           for k in ["ss", "sum", "mean", "ms", "var", "std", "rstd"]}
    scr["sq"] = p.sb("scr_sq", [P, D])
    dump = scr["sq"]
    den_h = [p.sb(f"den{i}", [P, 1]) for i in range(2)]
    den = p.sb("den", [P, 1])
    rden = p.sb("rden", [P, 1])

    ptA = p.ps("ptA", [P, P], BF16)
    ptB = p.ps("ptB", [P, P], BF16)
    pz = p.ps("pz", [P, 512])
    pcs = p.ps("pcs", [1, 512])
    pa0 = p.ps("pa0", [P, 512])
    pa1 = p.ps("pa1", [P, 512])
    pb = p.ps("pb", [P, P])

    xr = x.rearrange("(t p) d -> p t d", p=P)
    kpr = kp_in.rearrange("(t p) d -> p t d", p=P)
    vr = v_in.rearrange("(t p) d -> p t d", p=P)
    r1r = o_r1.rearrange("(t p) d -> p t d", p=P)

    s.dma("sync", R[:].rearrange("p (t d) -> p t d", t=NT), xr, sig="dld")
    s.dma("sync", ident_bf[:], ident, sig="dld")
    s.dma("sync", triu_f[:], triu_i, sig="dld")
    s.dma("sync", ones_st[:], ones_r, sig="dld")
    s.dma("sync", consts[:], consts_i, sig="dld")
    s.dma("sync", wstage[0:1, 0:D], bq, sig="dld")
    s.dma("sync", wstage[0:1, D:2 * D], bo, sig="dld")
    s.dma("sync", wstage[0:1, 2 * D:3 * D], kp_starts[0:1, 0:D], sig="dld")
    s.dma("sync", S[:].rearrange("p (k e) -> p k e", k=8),
          s_starts[0:D, :].rearrange("(k p) e -> p k e", p=P), sig="dld")
    s.wait("vector", "dld")
    s.op("vector", lambda e: e.tensor_copy(ones_bf[:], ones_st[:]))
    s.op("vector", lambda e: e.tensor_copy(ones_col_bf[:], consts[:, C_ONE:C_ONE + 1]))
    s.op("vector", lambda e: e.tensor_copy(bias_bf[:], wstage[0:1, 0:2 * D]))
    s.op("vector", lambda e: e.tensor_copy(z_run[:], wstage[0:1, 2 * D:3 * D]))
    s.op("vector", lambda e: e.tensor_copy(triu_bf[:], triu_f[:]), sig="cast_done")
    s.wait("sync", "cast_done", 1)
    s.dma("sync", wstage[0:1, 0:2 * D], kp_starts[0:1, D:3 * D], sig="kld")
    s.wait("vector", "kld")
    s.op("vector", lambda e: e.tensor_add(z_run[:], z_run[:],
                                          wstage[0:1, 0:D]))
    s.op("vector", lambda e: e.tensor_add(z_run[:], z_run[:],
                                          wstage[0:1, D:2 * D]))
    s.op("vector", lambda e: e.tensor_copy(z_bf[:], z_run[:]), sig="z_done")
    s.wait("sync", "z_done", 1)
    for wi, (wsrc, wdst) in enumerate([(wqT, wq_bf), (woT, wo_bf)]):
        for h in range(2):
            if (wi, h) != (0, 0):
                s.wait("sync", "cast_done", 1 + 2 * wi + h)
            s.dma("sync", wstage[:].rearrange("p (k o) -> p k o", k=4),
                  wsrc[512 * h:512 * (h + 1), :].rearrange("(k p) o -> p k o", p=P),
                  sig="wld")
            s.wait("vector", "wld")
            s.op("vector",
                 lambda e, wdst=wdst, h=h: e.tensor_copy(
                     wdst[:, h * 4 * D:(h + 1) * 4 * D], wstage[:]),
                 sig="cast_done")
    for j in (1, 2):
        for h in range(2):
            s.wait("sync", "cast_done" if (j, h) == (1, 0) else "s_add",
                   5 if (j, h) == (1, 0) else None)
            s.dma("sync", wstage[:].rearrange("p (k e) -> p k e", k=4),
                  s_starts[j * D + 512 * h: j * D + 512 * (h + 1), :]
                  .rearrange("(k p) e -> p k e", p=P), sig="sld")
            s.wait("vector", "sld")
            s.op("vector",
                 lambda e, h=h: e.tensor_add(
                     S[:, h * 4 * D:(h + 1) * 4 * D],
                     S[:, h * 4 * D:(h + 1) * 4 * D], wstage[:]),
                 sig="s_add")
    s.wait("gpsimd", "s_add", 4)
    s.op("gpsimd", lambda e: e.tensor_copy(S_bf[:], S[:]), sig="sbf_done")
    s.wait("scalar", "dld")

    for t0 in range(min(2, NT)):
        s.dma("sync", kp_t[t0][:], kpr[:, t0, :], sig="kpv_ld")
        s.dma("sync", v_t[t0][:], vr[:, t0, :], sig="kpv_ld")

    for t in range(NT):
        R_t = R[:, t * D:(t + 1) * D]
        kp_c = kp_t[t % 2]
        v_c = v_t[t % 2]
        emit_ln_tile(p, s, R_t, consts, scr, ln_bf[:], sig="ln_done")
        s.wait("tensor", "ln_done")
        emit_transpose8(p, s, ln_bf, lnfm, ident_bf, (ptA, ptB))
        s.wait("tensor", "tr_v2t")
        if t == 0:
            s.wait("tensor", "cast_done", 5)
        s.wait("tensor", "kv_evac")
        for half in range(2):
            pshalf = pa0 if half == 0 else pa1
            emit_proj_half(p, s, lnfm, wq_bf, bias_bf, ones_bf, pshalf, D, half,
                           sig="qproj")
            s.wait("scalar", "qproj")
            emit_phi_evac(p, s, pshalf, tmp1, tmp2,
                          qp_bf[:, half * 512:half * 512 + 512],
                          out_f32=qp_f32[:, half * 512:half * 512 + 512])
        s.op("vector", lambda e: e.tensor_copy(qp_bf[:, 0:1], qp_bf[:, 0:1]),
             sig="qp_done")
        s.wait("tensor", "qp_done")
        emit_transpose8(p, s, qp_bf, qpfm, ident_bf, (ptA, ptB))
        s.wait("tensor", "kpv_ld", 32 * (t + 1))
        s.wait("vector", "kpv_ld", 32 * (t + 1))
        emit_transpose8(p, s, kp_c, kpfm, ident_bf, (ptA, ptB))
        s.wait("tensor", "z_done")
        for half in range(2):
            s.op("tensor",
                 lambda e, half=half, b=kp_c: e.matmul(
                     pz[:], triu_bf[:],
                     b[:, half * 512:half * 512 + 512],
                     start=True, stop=False))
            s.op("tensor",
                 lambda e, half=half: e.matmul(
                     pz[:], ones_bf[0:1, :], z_bf[0:1, half * 512:half * 512 + 512],
                     start=False, stop=True),
                 sig="z_mm")
            s.wait("vector", "z_mm")
            s.op("vector",
                 lambda e, half=half, dh=den_h[half]: e.scalar_tensor_tensor(
                     dump[:, 0:512], pz[:], 1.0,
                     qp_f32[:, half * 512:half * 512 + 512],
                     ALU.mult, ALU.mult, accum_out=dh[:, 0:1]),
                 sig="den_acc")
            s.wait("tensor", "den_acc")
        s.wait("vector", "den_acc")
        s.opc("vector", lambda e: e.tensor_add(den[:], den_h[0][:], den_h[1][:]))
        s.opc("vector", lambda e: e.tensor_scalar(den[:], den[:], EPS_DEN, None,
                                                  ALU.add))
        s.opc("vector", lambda e: e.reciprocal(rden[:], den[:]))
        s.wait("vector", "hz_vector")
        for half in range(2):
            s.op("tensor",
                 lambda e, half=half, b=kp_c: e.matmul(
                     pcs[:], ones_col_bf[:, 0:1],
                     b[:, half * 512:half * 512 + 512],
                     start=True, stop=True),
                 sig="zcs_mm")
            s.wait("vector", "zcs_mm")
            s.op("vector",
                 lambda e, half=half: e.tensor_add(
                     z_run[0:1, half * 512:half * 512 + 512],
                     z_run[0:1, half * 512:half * 512 + 512], pcs[:]),
                 sig="zcs_evac")
            s.wait("tensor", "zcs_evac")
        s.op("vector", lambda e: e.tensor_copy(z_bf[:], z_run[:]), sig="z_done")
        s.wait("tensor", "sbf_done", 1 + t)
        for half in range(2):
            pshalf = pa0 if half == 0 else pa1
            for k in range(8):
                s.op("tensor",
                     lambda e, k=k, half=half, ps_=pshalf: e.matmul(
                         ps_[:], qpfm[:, k * P:(k + 1) * P],
                         S_bf[:, k * D + half * 512:k * D + half * 512 + 512],
                         start=(k == 0), stop=False))
        for k in range(8):
            s.op("tensor",
                 lambda e, k=k: e.matmul(
                     pb[:], kpfm[:, k * P:(k + 1) * P],
                     qpfm[:, k * P:(k + 1) * P],
                     start=(k == 0), stop=(k == 7)),
                 sig="at_mm" if k == 7 else None)
        s.wait("vector", "at_mm")
        s.op("vector", lambda e: e.tensor_mul(at_bf[:], pb[:], triu_f[:]),
             sig="at_evac")
        s.wait("tensor", "at_evac")
        for half in range(2):
            pshalf = pa0 if half == 0 else pa1
            s.op("tensor",
                 lambda e, half=half, ps_=pshalf, b=v_c: e.matmul(
                     ps_[:], at_bf[:],
                     b[:, half * 512:half * 512 + 512],
                     start=False, stop=True),
                 sig="num_mm")
        s.wait("vector", "num_mm")
        for half in range(2):
            pshalf = pa0 if half == 0 else pa1
            s.op("vector",
                 lambda e, half=half, ps_=pshalf: e.tensor_scalar(
                     o_bf[:, half * 512:half * 512 + 512], ps_[:], rden[:, 0:1],
                     None, ALU.mult),
                 sig="o_done")
        s.wait("tensor", "o_done")
        emit_transpose8(p, s, o_bf, ofm, ident_bf, (ptA, ptB))
        for half in range(2):
            pshalf = pa0 if half == 0 else pa1
            emit_proj_half(p, s, ofm, wo_bf, bias_bf[0:1, D:2 * D], ones_bf,
                           pshalf, D, half, sig="oproj")
            s.wait("vector", "oproj")
            s.op("vector",
                 lambda e, half=half, ps_=pshalf, Rt=R_t: e.tensor_add(
                     Rt[:, half * 512:half * 512 + 512],
                     Rt[:, half * 512:half * 512 + 512], ps_[:]),
                 sig="r_done")
        s.wait("tensor", "r_done")
        for k in range(8):
            for half in range(2):
                pshalf = pa0 if half == 0 else pa1
                s.op("tensor",
                     lambda e, k=k, ps_=pshalf, half=half, kp_b=kp_c, v_b=v_c:
                     e.matmul(ps_[:], kp_b[:, k * P:(k + 1) * P],
                              v_b[:, half * 512:half * 512 + 512],
                              start=True, stop=True),
                     sig="kv_mm")
            s.wait("vector", "kv_mm")
            for half in range(2):
                pshalf = pa0 if half == 0 else pa1
                dst = S[:, k * D + half * 512: k * D + half * 512 + 512]
                s.op("vector",
                     lambda e, d=dst, ps_=pshalf: e.tensor_add(d, d, ps_[:]),
                     sig="kv_evac")
            if k < 7:
                s.wait("tensor", "kv_evac")
        if t < NT - 1:
            s.wait("gpsimd", "kv_evac")
            s.op("gpsimd", lambda e: e.tensor_copy(S_bf[:], S[:]),
                 sig="sbf_done")
        s.wait("sync", "r_done")
        s.dma("sync", r1r[:, t, :], R_t, sig="rst")
        if t + 2 < NT:
            s.wait("sync", "kv_mm")
            s.dma("sync", kp_t[(t + 2) % 2][:], kpr[:, t + 2, :], sig="kpv_ld")
            s.dma("sync", v_t[(t + 2) % 2][:], vr[:, t + 2, :], sig="kpv_ld")

    s.wait("gpsimd", "rst")
    return p.finish()


def run_l0a_finish(x_shards, Wq, bq, Wo, bo, kp_l, v_l, sl_l, ks_l):
    nc = build_l0a_finish()
    wqT = np.ascontiguousarray(Wq.T)
    woT = np.ascontiguousarray(Wo.T)
    base = dict(wqT=wqT, woT=woT, bq=bq.reshape(1, D), bo=bo.reshape(1, D),
                ident_bf=to_bf16(_hc["ident_bf"]), triu=_hc["triu_bf"],
                ones_row=_hc["ones_row"], consts=_hc["consts"])
    zeros_s = np.zeros((D, D), np.float32)
    zeros_k = np.zeros((1, D), np.float32)
    in_maps = []
    for c in range(NCORES):
        q = c % 4
        grp = (c // 4) * 4
        preds = [grp + j for j in range(q)]
        sst = [np.asarray(sl_l[j], np.float32) for j in preds]
        kst = [np.asarray(ks_l[j], np.float32).reshape(1, D) for j in preds]
        while len(sst) < 3:
            sst.append(zeros_s)
            kst.append(zeros_k)
        in_maps.append(dict(base, x=x_shards[c], kp=kp_l[c], v=v_l[c],
                            s_starts=np.concatenate(sst, 0),
                            kp_starts=np.concatenate(kst, 1)))
    res = run_bass_kernel_spmd(nc, in_maps, core_ids=list(range(NCORES)))
    return res.results



DBG_L0F = False
NE0 = 16
NG = 4
GTOK = 512
SW_SCALE = 8.0 / 1.702


def build_l0_ffn():
    p = Prog("l0_ffn")
    s = p.s
    r1 = p.inp("r1", [TOK, D])
    scale_fm_i = p.inp("scale_fm", [P, 8])
    gate_wT = p.inp("gate_wT", [D, 16])
    gate_b = p.inp("gate_b", [1, 16])
    w1p = p.inp("w1p", [NE0 * D, D])
    b1p = p.inp("b1p", [NE0, D])
    w2p = p.inp("w2p", [NE0 * D, D])
    b2_i = p.inp("b2", [NE0, D])
    ident = p.inp("ident_bf", [P, P], BF16)
    ident_f = p.inp("ident_f", [P, P])
    ones_r = p.inp("ones_row", [1, P])
    ones5_i = p.inp("ones_512", [1, GTOK])
    consts_i = p.inp("consts", [P, 8])

    o_r2 = p.out("r2", [TOK, D])
    route_dram = p.nc.dram_tensor("route_dram", [16, TOK], BF16).ap()

    R = p.sb("R", [P, 2 * D])
    acc = p.sb("acc", [P, 8 * TOK])
    t_bf = p.sb("t_bf", [P, 8 * TOK], BF16)
    wstage = p.sb("wstage", [P, 4 * D])
    w1_bf = p.sb("w1_bf", [P, 8 * D], BF16)
    w2_bf = p.sb("w2_bf", [P, 8 * D], BF16)
    tg32 = None
    s_fm = p.sb("s_fm", [P, 8 * GTOK], BF16)
    ident_bf = p.sb("identb", [P, P], BF16)
    ident_f32 = p.sb("identf", [P, P])
    ones_bf = p.sb("ones_bf", [1, P], BF16)
    ones_g_bf = p.sb("ones_g_bf", [1, GTOK], BF16)
    ones_g_st = p.sb("ones_g_st", [1, GTOK])
    ones_st = p.sb("ones_st", [1, P])
    consts = p.sb("consts_s", [P, 8])
    scale_fm = p.sb("scale_fm_s", [P, 8])
    gwT = p.sb("gwT", [P, 8 * 16])
    gb = p.sb("gb", [1, 16])
    b2_bf = p.sb("b2_bf", [16, D], BF16)
    b2_st = p.sb("b2_st", [16, D])
    b1_bf = p.sb("b1_bf", [1, D], BF16)
    route_fm = p.sb("route_fm", [16, TOK], BF16)
    route_row = p.sb("route_row", [1, TOK], BF16)
    bc_bf = p.sb("bc_bf", [P, GTOK], BF16)
    g_sb = p.sb("g_sb", [P, 16])
    g_w = [p.sb(f"g_w{i}", [P, 16]) for i in range(2)]
    gd = p.sb("gd", [P, 16])
    gmask = p.sb("gmask", [P, 16])
    ge_exp = p.sb("ge_exp", [P, 16])
    gem = p.sb("gem", [P, 16])
    route32 = p.sb("route32", [P, 16])
    route_b = p.sb("route_b", [P, 16], BF16)
    m1 = p.sb("m1", [P, 1]); m4 = p.sb("m4", [P, 1]); mi = p.sb("mi", [P, 1])
    negm1 = p.sb("negm1", [P, 1])
    ssum = p.sb("ssum", [P, 1]); rs = p.sb("rs", [P, 1])
    t_tm = p.sb("t_tm", [P, D])
    ln2_t = p.sb("ln2_t", [P, D])
    m_sb = p.sb("m_sb", [P, 512])
    sg_bf = p.sb("sg_bf", [P, 512], BF16)
    y_tmp = p.sb("y_tmp", [P, 512])
    scr = {k: p.sb("scr_" + k, [P, 1])
           for k in ["ss", "sum", "mean", "ms", "var", "std", "rstd"]}
    scr["sq"] = p.sb("scr_sq", [P, D])

    tg32 = acc
    ptA = p.ps("ptA", [P, P], BF16)
    ptB = p.ps("ptB", [P, P], BF16)
    ptC = p.ps("ptC", [P, P])
    ptD = p.ps("ptD", [P, P])
    pm = p.ps("pm", [P, 512])
    po = p.ps("po", [P, 512])
    pg = p.ps("pg", [P, 16])
    pbc = p.ps("pbc", [P, GTOK])

    r1r = r1.rearrange("(t p) d -> p t d", p=P)
    r2r = o_r2.rearrange("(t p) d -> p t d", p=P)
    zbc16 = consts[:, C_ZERO:C_ZERO + 1].broadcast_to([P, 16])

    s.dma("sync", ident_bf[:], ident, sig="dld")
    s.dma("sync", ident_f32[:], ident_f, sig="dld")
    s.dma("sync", ones_st[:], ones_r, sig="dld")
    s.dma("sync", ones_g_st[:], ones5_i, sig="dld")
    s.dma("sync", consts[:], consts_i, sig="dld")
    s.dma("sync", scale_fm[:], scale_fm_i, sig="dld")
    s.dma("sync", gwT[:].rearrange("p (k e) -> p k e", k=8),
          gate_wT.rearrange("(k p) e -> p k e", p=P), sig="dld")
    s.dma("sync", gb[:], gate_b, sig="dld")
    s.dma("sync", b2_st[:], b2_i, sig="dld")
    s.dma("sync", R[:, 0:D], r1r[:, 0, :], sig="p1ld")
    s.dma("sync", R[:, D:2 * D], r1r[:, 1, :], sig="p1ld")
    s.wait("vector", "dld")
    s.op("vector", lambda e: e.tensor_copy(ones_bf[:], ones_st[:]))
    s.op("vector", lambda e: e.tensor_copy(ones_g_bf[:], ones_g_st[:]))
    s.op("vector", lambda e: e.tensor_copy(b2_bf[:], b2_st[:]), sig="init_cast")
    s.wait("scalar", "dld")

    for g in range(NG):
        for ti in range(4):
            t = g * 4 + ti
            R_t = R[:, (t % 2) * D:(t % 2 + 1) * D]
            s.wait("vector", "p1ld", 16 * (t + 1))
            s.wait("scalar", "p1ld", 16 * (t + 1))
            emit_ln_tile(p, s, R_t, consts, scr, None, out_f32=ln2_t[:],
                         sig="ln_done")
            if t + 2 < NT:
                s.wait("sync", "ln_done", t + 1)
                s.dma("sync", R[:, (t % 2) * D:(t % 2 + 1) * D],
                      r1r[:, t + 2, :], sig="p1ld")
            s.wait("scalar", "ln_done")
            emit_ln_tile(p, s, ln2_t[:], consts, scr, None, out_f32=t_tm[:],
                         rms_only=True, sig="rms_done")
            s.wait("tensor", "rms_done")

            def evac_scale(e, k, bank, ti=ti):
                return e.tensor_scalar(
                    tg32[:, k * GTOK + ti * P: k * GTOK + ti * P + P],
                    bank[:], scale_fm[:, k:k + 1], None, ALU.mult)
            emit_transpose8(p, s, t_tm, None, ident_f32, (ptC, ptD), ch="trf",
                            evac=evac_scale)
            s.wait("vector", "trf_v2t")
            for k in range(8):
                s.op("vector",
                     lambda e, k=k, g=g, ti=ti: e.tensor_copy(
                         t_bf[:, k * TOK + g * GTOK + ti * P:
                              k * TOK + g * GTOK + ti * P + P],
                         tg32[:, k * GTOK + ti * P: k * GTOK + ti * P + P]),
                     sig="tbf_done")
            s.wait("tensor", "tbf_done")
            for k in range(8):
                s.op("tensor",
                     lambda e, k=k, ti=ti: e.matmul(
                         pg[:], tg32[:, k * GTOK + ti * P: k * GTOK + ti * P + P],
                         gwT[:, k * 16:(k + 1) * 16], start=(k == 0), stop=False))
            s.op("tensor",
                 lambda e: e.matmul(pg[:], ones_st[0:1, :], gb[0:1, :],
                                    start=False, stop=True), sig="g_mm")
            s.wait("vector", "g_mm")
            s.opc("vector", lambda e: e.tensor_copy(g_sb[:], pg[:]))
            s.wait("tensor", "hz_vector")
            cur = g_sb
            for it in range(4):
                dst_m = m1 if it == 0 else (m4 if it == 3 else mi)
                s.opc("vector", lambda e, c=cur, dm=dst_m: e.reduce_max(
                    dm[:], c[:], axis=AXX))
                if it < 3:
                    nxt = g_w[it % 2]
                    s.opc("vector", lambda e, c=cur, dm=dst_m: e.tensor_scalar(
                        gd[:], c[:], dm[:, 0:1], None, ALU.subtract))
                    s.opc("vector", lambda e: e.tensor_tensor(
                        gmask[:], gd[:], zbc16, ALU.is_ge))
                    s.opc("vector", lambda e, c=cur, n=nxt: e.scalar_tensor_tensor(
                        n[:], gmask[:], -1e9, c[:], ALU.mult, ALU.add))
                    cur = nxt
            s.opc("vector", lambda e: e.tensor_scalar(gd[:], g_sb[:], m4[:, 0:1],
                                                      None, ALU.subtract))
            s.opc("vector", lambda e: e.tensor_tensor(gmask[:], gd[:], zbc16,
                                                      ALU.is_ge))
            s.opc("vector", lambda e: e.tensor_scalar(negm1[:], m1[:], -1.0,
                                                      None, ALU.mult))
            s.op("vector", lambda e: e.tensor_copy(gd[:], g_sb[:]), sig="g_v2s")
            s.wait("scalar", "g_v2s")
            s.op("scalar", lambda e: e.activation(ge_exp[:], g_sb[:], AF.Exp,
                                                  bias=negm1[:, 0:1]),
                 sig="g_s2v")
            s.wait("vector", "g_s2v")
            s.opc("vector", lambda e: e.tensor_mul(gem[:], ge_exp[:], gmask[:]))
            s.opc("vector", lambda e: e.reduce_sum(ssum[:], gem[:], axis=AXX))
            s.opc("vector", lambda e: e.reciprocal(rs[:], ssum[:]))
            s.opc("vector", lambda e: e.tensor_scalar(route32[:], gem[:],
                                                      rs[:, 0:1], None, ALU.mult))
            s.opc("vector", lambda e: e.tensor_copy(route_b[:], route32[:]))
            s.wait("vector", "hz_vector")
            s.op("vector", lambda e: e.tensor_copy(route_b[:, 0:1],
                                                   route_b[:, 0:1]),
                 sig="route_done")
            s.wait("tensor", "route_done")
            s.wait("tensor", "tr_v2t")
            s.op("tensor", lambda e: e.transpose(ptA[0:16, 0:P], route_b[:],
                                                 ident_bf[:]), sig="tr_t2v")
            s.wait("vector", "tr_t2v")
            s.op("vector",
                 lambda e, t=t: e.tensor_copy(
                     route_fm[0:16, t * P:(t + 1) * P], ptA[0:16, 0:P]),
                 sig="tr_v2t")

    s.wait("sync", "tr_v2t")
    s.dma("sync", route_dram, route_fm[:], sig="route_st")

    for e_i in range(NE0):
        if e_i > 0:
            s.wait("gpsimd", "m_mm", 32 * e_i)
            s.wait("gpsimd", "y_mm", 32 * e_i)
            s.wait("vector", "m_mm", 32 * e_i)
        for h2 in range(2):
            s.wait("sync", "wcast", 4 * e_i + h2)
            if h2 == 0:
                s.wait("sync", "b1cast", e_i)
            s.dma("sync", wstage[:].rearrange("p (k i) -> p k i", k=4),
                  w1p[e_i * D + 512 * h2: e_i * D + 512 * (h2 + 1), :]
                  .rearrange("(k p) i -> p k i", p=P), sig="wld")
            s.wait("gpsimd", "wld")
            s.op("gpsimd",
                 lambda e, h2=h2: e.tensor_copy(
                     w1_bf[:, h2 * 4 * D:(h2 + 1) * 4 * D], wstage[:]),
                 sig="wcast")
        for h2 in range(2):
            s.wait("sync", "wcast", 4 * e_i + 2 + h2)
            s.dma("sync", wstage[:].rearrange("p (k i) -> p k i", k=4),
                  w2p[e_i * D + 512 * h2: e_i * D + 512 * (h2 + 1), :]
                  .rearrange("(k p) i -> p k i", p=P), sig="wld")
            s.wait("gpsimd", "wld")
            s.op("gpsimd",
                 lambda e, h2=h2: e.tensor_copy(
                     w2_bf[:, h2 * 4 * D:(h2 + 1) * 4 * D], wstage[:]),
                 sig="wcast")
        s.wait("sync", "wcast", 4 * (e_i + 1))
        s.dma("sync", wstage[0:1, 0:D], b1p[e_i:e_i + 1, :], sig="b1ld")
        s.wait("vector", "b1ld")
        s.op("vector", lambda e: e.tensor_copy(b1_bf[:], wstage[0:1, 0:D]),
             sig="b1cast")

        s.wait("sync", "route_st", 16)
        if e_i > 0:
            s.wait("sync", "bc_mm", 4 * e_i)
        s.dma("sync", route_row[:], route_dram[e_i:e_i + 1, :], sig="route_ld")
        for g in range(NG):
            if e_i == 0:
                s.wait("tensor", "tbf_done", 32 * (g + 1))
            s.wait("tensor", "bc_done")
            s.wait("tensor", "route_ld", 16 * (e_i + 1))
            s.op("tensor",
                 lambda e, g=g: e.matmul(
                     pbc[:], ones_bf[0:1, :],
                     route_row[0:1, g * GTOK:(g + 1) * GTOK],
                     start=True, stop=True),
                 sig="bc_mm")
            s.wait("vector", "bc_mm")
            s.op("vector", lambda e: e.tensor_scalar(bc_bf[:], pbc[:], SW_SCALE,
                                                     None, ALU.mult),
                 sig="bc_done")
            if g == 0:
                s.wait("tensor", "wcast", 4 * e_i + 2)
                s.wait("tensor", "b1cast", e_i + 1)
            for ich in range(8):
                if not (e_i == 0 and g == 0 and ich == 0):
                    s.wait("tensor", "m_evac")
                for k in range(8):
                    s.op("tensor",
                         lambda e, k=k, ich=ich, g=g: e.matmul(
                             pm[:], w1_bf[:, k * D + ich * P: k * D + ich * P + P],
                             t_bf[:, k * TOK + g * GTOK: k * TOK + (g + 1) * GTOK],
                             start=(k == 0), stop=False))
                s.op("tensor",
                     lambda e, ich=ich: e.matmul(
                         pm[:], b1_bf[0:1, ich * P:(ich + 1) * P],
                         ones_g_bf[0:1, :],
                         start=False, stop=True),
                     sig="m_mm")
                s.wait("vector", "m_mm")
                s.wait("vector", "s_done")
                s.op("vector", lambda e: e.tensor_scalar(m_sb[:], pm[:], 7.0,
                                                         None, ALU.min),
                     sig="m_evac")
                s.wait("scalar", "m_evac")
                s.op("scalar",
                     lambda e, ich=ich: e.activation(
                         s_fm[:, ich * GTOK:(ich + 1) * GTOK], m_sb[:],
                         AF.Silu, scale=1.702),
                     sig="s_done")
            if g == 0:
                s.wait("tensor", "wcast", 4 * (e_i + 1))
            s.wait("tensor", "s_done")
            for och in range(8):
                if not (e_i == 0 and g == 0 and och < 1):
                    s.wait("tensor", "y_evac")
                for k in range(8):
                    s.op("tensor",
                         lambda e, k=k, och=och: e.matmul(
                             po[:], w2_bf[:, k * D + och * P: k * D + och * P + P],
                             s_fm[:, k * GTOK:(k + 1) * GTOK],
                             start=(k == 0), stop=(k == 7)),
                         sig="y_mm" if k == 7 else None)
                s.wait("vector", "y_mm")
                dst = acc[:, och * TOK + g * GTOK: och * TOK + (g + 1) * GTOK]
                if e_i == 0:
                    s.op("vector",
                         lambda e, d=dst: e.tensor_mul(d, po[:], bc_bf[:]),
                         sig="y_evac")
                else:
                    s.op("vector",
                         lambda e, d=dst: e.tensor_mul(y_tmp[:], po[:], bc_bf[:]))
                    s.op("vector",
                         lambda e, d=dst: e.tensor_add(d, d, y_tmp[:]),
                         sig="y_evac")

    s.wait("tensor", "init_cast")
    for g in range(NG):
        for och in range(8):
            s.wait("tensor", "y_evac")
            s.op("tensor",
                 lambda e, g=g, och=och: e.matmul(
                     po[:], b2_bf[0:16, och * P:(och + 1) * P],
                     route_fm[0:16, g * GTOK:(g + 1) * GTOK],
                     start=True, stop=True),
                 sig="y_mm")
            s.wait("vector", "y_mm")
            dst = acc[:, och * TOK + g * GTOK: och * TOK + (g + 1) * GTOK]
            s.op("vector", lambda e, d=dst: e.tensor_add(d, d, po[:]),
                 sig="y_evac")

    s.dma("sync", R[:, 0:D], r1r[:, 0, :], sig="p3ld")
    s.dma("sync", R[:, D:2 * D], r1r[:, 1, :], sig="p3ld")
    for t in range(NT):
        R_t = R[:, (t % 2) * D:(t % 2 + 1) * D]
        s.wait("vector", "p3ld", 16 * (t + 1))
        s.wait("scalar", "p3ld", 16 * (t + 1))
        emit_ln_tile(p, s, R_t, consts, scr, None, out_f32=ln2_t[:],
                     sig="ln_done")
        s.wait("vector", "ln_done")
        s.op("vector", lambda e, Rt=R_t: e.tensor_add(Rt, Rt, ln2_t[:]),
             sig="racc")
        g, ti = divmod(t, 4)
        if t == 0:
            s.wait("tensor", "y_evac")
        s.wait("tensor", "trf_v2t")
        for k in range(8):
            bank = (ptC, ptD)[k % 2]
            if k >= 2:
                s.wait("tensor", "trf_v2t")
            s.op("tensor",
                 lambda e, k=k, bank=bank, g=g, ti=ti: e.transpose(
                     bank[:],
                     acc[:, k * TOK + g * GTOK + ti * P:
                         k * TOK + g * GTOK + ti * P + P],
                     ident_f32[:]),
                 sig="trf_t2v")
            s.wait("vector", "trf_t2v")
            s.op("vector",
                 lambda e, k=k, bank=bank, Rt=R_t: e.tensor_add(
                     Rt[:, k * P:(k + 1) * P], Rt[:, k * P:(k + 1) * P],
                     bank[:]),
                 sig="trf_v2t")
        s.wait("sync", "trf_v2t")
        s.dma("sync", r2r[:, t, :], R_t, sig="rst")
        if t + 2 < NT:
            s.wait("sync", "rst", 16 * (t + 1))
            s.dma("sync", R[:, (t % 2) * D:(t % 2 + 1) * D],
                  r1r[:, t + 2, :], sig="p3ld")

    if DBG_L0F:
        dbg_t = p.out("dbg_t", [P, 8 * TOK], BF16)
        dbg_route = p.out("dbg_route", [16, TOK], BF16)
        dbg_acc = p.out("dbg_acc", [P, 8 * TOK])
        s.dma("sync", dbg_t, t_bf[:], sig="rst")
        s.dma("sync", dbg_route, route_fm[:], sig="rst")
        s.dma("sync", dbg_acc, acc[:], sig="rst")
    s.wait("gpsimd", "rst")
    return p.finish()


def run_l0_ffn(r1_l, scale, gate_w, gate_b, mlp1_w, mlp1_b, mlp2_w, mlp2_b):
    nc = build_l0_ffn()
    scale_fm = np.ascontiguousarray(scale.reshape(8, P).T)
    gate_wT = np.ascontiguousarray(gate_w.T)
    w1p = np.ascontiguousarray(
        mlp1_w[:, ::2, :].transpose(0, 2, 1)).reshape(NE0 * D, D)
    b1p = np.ascontiguousarray(mlp1_b[:, ::2])
    w2p = np.ascontiguousarray(mlp2_w.transpose(0, 2, 1)).reshape(NE0 * D, D)
    base = dict(scale_fm=scale_fm, gate_wT=gate_wT,
                gate_b=gate_b.reshape(1, 16), w1p=w1p, b1p=b1p, w2p=w2p,
                b2=mlp2_b, ident_bf=to_bf16(_hc["ident_bf"]),
                ident_f=_hc["ident_bf"], ones_row=_hc["ones_row"],
                ones_512=_hc["ones_512"], consts=_hc["consts"])
    in_maps = [dict(base, r1=np.ascontiguousarray(r1_l[c]))
               for c in range(NCORES)]
    res = run_bass_kernel_spmd(nc, in_maps, core_ids=list(range(NCORES)))
    return res.results



def build_l1a_qkv():
    p = Prog("l1a_qkv")
    s = p.s
    r2 = p.inp("r2", [TOK, D])
    w_qkv = p.inp("w_qkv", [D, 3 * D])
    b_qkv = p.inp("b_qkv", [1, 3 * D])
    cosr_i = p.inp("cosr", [TOK, 512])
    sinr_i = p.inp("sinr", [TOK, 512])
    ident = p.inp("ident_bf", [P, P], BF16)
    ones_r = p.inp("ones_row", [1, P])
    consts_i = p.inp("consts", [P, 8])

    o_rq = p.out("rq_fm", [D, TOK], BF16)
    o_q = p.out("q_fm", [D, TOK], BF16)
    o_rk = p.out("rk_fm", [D, TOK], BF16)
    o_k = p.out("k_fm", [D, TOK], BF16)
    o_v = p.out("v_tm", [TOK, D], BF16)

    R = p.sb("R", [P, 2 * D])
    wstage = p.sb("wstage", [P, 4 * D])
    wq_bf = p.sb("wq_bf", [P, 8 * 3 * D], BF16)
    bias_bf = p.sb("bias_bf", [1, 3 * D], BF16)
    ident_bf = p.sb("identb", [P, P], BF16)
    ones_bf = p.sb("ones_bf", [1, P], BF16)
    ones_st = p.sb("ones_st", [1, P])
    consts = p.sb("consts_s", [P, 8])
    ln_bf = p.sb("ln_bf", [P, D], BF16)
    lnfm = p.sb("lnfm", [P, 8 * P], BF16)
    q_tm = p.sb("q_tm", [P, D])
    k_tm = p.sb("k_tm", [P, D])
    rq_tm = p.sb("rq_tm", [P, D])
    rk_tm = p.sb("rk_tm", [P, D])
    v_bf = p.sb("v_bf", [P, D], BF16)
    cast_bf = p.sb("cast_bf", [P, D], BF16)
    fmbuf = p.sb("fmbuf", [P, 8 * P], BF16)
    cos_t = p.sb("cos_t", [P, 512])
    sin_t = p.sb("sin_t", [P, 512])
    mt1 = p.sb("mt1", [P, 512])
    mt2 = p.sb("mt2", [P, 512])
    tmp1 = p.sb("tmp1", [P, 512])
    tmp2 = p.sb("tmp2", [P, 512])
    scr = {k: p.sb("scr_" + k, [P, 1])
           for k in ["ss", "sum", "mean", "ms", "var", "std", "rstd"]}
    scr["sq"] = p.sb("scr_sq", [P, D])

    ptA = p.ps("ptA", [P, P], BF16)
    ptB = p.ps("ptB", [P, P], BF16)
    pa0 = p.ps("pa0", [P, 512])
    pa1 = p.ps("pa1", [P, 512])

    r2r = r2.rearrange("(t p) d -> p t d", p=P)
    vr = o_v.rearrange("(t p) d -> p t d", p=P)
    cosr = cosr_i.rearrange("(t p) j -> p t j", p=P)
    sinr = sinr_i.rearrange("(t p) j -> p t j", p=P)
    fm_outs = {"rq": o_rq, "q": o_q, "rk": o_rk, "k": o_k}
    fm_r = {k: v.rearrange("(c p) T -> p c T", p=P) for k, v in fm_outs.items()}

    s.dma("sync", ident_bf[:], ident, sig="dld")
    s.dma("sync", ones_st[:], ones_r, sig="dld")
    s.dma("sync", consts[:], consts_i, sig="dld")
    s.dma("sync", wstage[0:1, 0:3 * D], b_qkv, sig="dld")
    s.dma("sync", R[:, 0:D], r2r[:, 0, :], sig="p1ld")
    s.dma("sync", R[:, D:2 * D], r2r[:, 1, :], sig="p1ld")
    s.wait("vector", "dld")
    s.op("vector", lambda e: e.tensor_copy(ones_bf[:], ones_st[:]))
    s.op("vector", lambda e: e.tensor_copy(bias_bf[:], wstage[0:1, 0:3 * D]),
         sig="cast_done")
    npieces = 0
    for rh in range(2):
        for ch in range(3):
            s.wait("sync", "cast_done", 1 + npieces)
            s.dma("sync", wstage[:].rearrange("p (k o) -> p k o", k=4),
                  w_qkv[512 * rh:512 * (rh + 1), ch * D:(ch + 1) * D]
                  .rearrange("(k p) o -> p k o", p=P), sig="wld")
            s.wait("vector", "wld")
            def cast_piece(e, rh=rh, ch=ch):
                for kk in range(4):
                    e.tensor_copy(
                        wq_bf[:, (4 * rh + kk) * 3 * D + ch * D:
                              (4 * rh + kk) * 3 * D + (ch + 1) * D],
                        wstage[:, kk * D:(kk + 1) * D])
                return e.tensor_copy(bias_bf[:, 0:1], bias_bf[:, 0:1])
            s.op("vector", cast_piece, sig="cast_done")
            npieces += 1
    s.wait("scalar", "p1ld", 16)

    for t in range(NT):
        R_t = R[:, (t % 2) * D:(t % 2 + 1) * D]
        s.wait("vector", "p1ld", 16 * (t + 1))
        s.wait("scalar", "p1ld", 16 * (t + 1))
        emit_ln_tile(p, s, R_t, consts, scr, ln_bf[:], sig="ln_done")
        if t + 2 < NT:
            s.wait("sync", "ln_done", t + 1)
            s.dma("sync", R[:, (t % 2) * D:(t % 2 + 1) * D],
                  r2r[:, t + 2, :], sig="p1ld")
        s.wait("sync", "rope_done", 2 * t)
        s.wait("vector", "vst", 16 * t)
        s.dma("sync", cos_t[:], cosr[:, t, :], sig="cld")
        s.dma("sync", sin_t[:], sinr[:, t, :], sig="cld")
        s.wait("tensor", "ln_done")
        emit_transpose8(p, s, ln_bf, lnfm, ident_bf, (ptA, ptB))
        s.wait("tensor", "tr_v2t")
        if t == 0:
            s.wait("tensor", "cast_done", 7)
        for half in range(6):
            pshalf = pa0 if half % 2 == 0 else pa1
            s.wait("tensor", "phi_v2s")
            s.wait("tensor", "evac_done")
            for k in range(8):
                s.op("tensor",
                     lambda e, k=k, half=half, ps_=pshalf: e.matmul(
                         ps_[:], lnfm[:, k * P:(k + 1) * P],
                         wq_bf[:, k * 3 * D + half * 512:
                               k * 3 * D + half * 512 + 512],
                         start=(k == 0), stop=False))
            s.op("tensor",
                 lambda e, half=half, ps_=pshalf: e.matmul(
                     ps_[:], ones_bf[0:1, :],
                     bias_bf[0:1, half * 512:half * 512 + 512],
                     start=False, stop=True),
                 sig="proj_mm")
            if half < 4:
                dst = (q_tm if half < 2 else k_tm)
                s.wait("scalar", "proj_mm")
                emit_phi_evac(p, s, pshalf, tmp1, tmp2, None,
                              out_f32=dst[:, (half % 2) * 512:
                                          (half % 2) * 512 + 512])
            else:
                s.wait("vector", "proj_mm")
                s.op("vector",
                     lambda e, half=half, ps_=pshalf: e.tensor_copy(
                         v_bf[:, (half - 4) * 512:(half - 4) * 512 + 512],
                         ps_[:]),
                     sig="evac_done")
        s.wait("sync", "evac_done")
        s.dma("sync", vr[:, t, :], v_bf[:], sig="vst")
        s.wait("vector", "cld", 32 * (t + 1))
        for src_tm, rot_tm_ in ((q_tm, rq_tm), (k_tm, rk_tm)):
            x1 = src_tm[:, 0:D:2]
            x2 = src_tm[:, 1:D:2]
            s.op("vector", lambda e, x1=x1: e.tensor_mul(mt1[:], x1, cos_t[:]))
            s.op("vector", lambda e, x2=x2: e.tensor_mul(mt2[:], x2, sin_t[:]))
            s.op("vector", lambda e, r=rot_tm_: e.tensor_sub(
                r[:, 0:D:2], mt1[:], mt2[:]))
            s.op("vector", lambda e, x1=x1: e.tensor_mul(mt1[:], x1, sin_t[:]))
            s.op("vector", lambda e, x2=x2: e.tensor_mul(mt2[:], x2, cos_t[:]))
            s.op("vector", lambda e, r=rot_tm_: e.tensor_add(
                r[:, 1:D:2], mt1[:], mt2[:]), sig="rope_done")
        for name, src_tm in (("rq", rq_tm), ("q", q_tm), ("rk", rk_tm),
                             ("k", k_tm)):
            s.wait("vector", "tr_t2v")
            s.wait("vector", "fst")
            s.op("vector", lambda e, st_=src_tm: e.tensor_copy(cast_bf[:], st_[:]),
                 sig="castq")
            s.wait("tensor", "castq")
            emit_transpose8(p, s, cast_bf, fmbuf, ident_bf, (ptA, ptB))
            s.wait("sync", "tr_v2t")
            s.dma("sync", fm_r[name][:, :, t * P:(t + 1) * P],
                  fmbuf[:].rearrange("p (c q) -> p c q", c=8), sig="fst")
    s.wait("gpsimd", "vst")
    return p.finish()



NH = 16
HD = 64
SEQ = 4 * TOK
NKT = SEQ // P


def build_l1a_attn():
    p = Prog("l1a_attn")
    s = p.s
    r2 = p.inp("r2", [TOK, D])
    rqf = p.inp("rqf", [D, TOK], BF16)
    qf = p.inp("qf", [D, TOK], BF16)
    rk4 = p.inp("rk4", [4 * D, TOK], BF16)
    k4 = p.inp("k4", [4 * D, TOK], BF16)
    v4 = p.inp("v4", [SEQ, D], BF16)
    w_out = p.inp("w_out", [D, D])
    b_out = p.inp("b_out", [1, D])
    ident = p.inp("ident_bf", [P, P], BF16)
    ones_r = p.inp("ones_row", [1, P])
    consts_i = p.inp("consts", [P, 8])

    o_r3 = p.out("r3", [TOK, D])

    R = p.sb("R", [P, NT * D])
    attn_fm = p.sb("attn_fm", [P, 8 * TOK], BF16)
    kq = p.sb("kq", [P, SEQ], BF16)
    v_h = p.sb("v_h", [P, NKT * HD], BF16)
    rq = p.sb("rq", [P, TOK], BF16)
    wstage = p.sb("wstage", [P, 4 * D])
    wo_bf = p.sb("wo_bf", [P, 8 * D], BF16)
    bias_bf = p.sb("bias_bf", [1, D], BF16)
    ident_bf = p.sb("identb", [P, P], BF16)
    ones_bf = p.sb("ones_bf", [1, P], BF16)
    ones_st = p.sb("ones_st", [1, P])
    consts = p.sb("consts_s", [P, 8])
    rc = [p.sb(f"rc{i}", [P, 512]) for i in range(2)]
    wT = [p.sb(f"wT{i}", [P, 512], BF16) for i in range(2)]

    psN = [p.ps(f"psN{i}", [P, 512]) for i in range(2)]
    psD = [p.ps(f"psD{i}", [P, 512]) for i in range(2)]
    psO = p.ps("psO", [P, 512])
    ptA = p.ps("ptA", [P, 512])
    ptB = p.ps("ptB", [P, 512])

    r2r = r2.rearrange("(t p) d -> p t d", p=P)
    r3r = o_r3.rearrange("(t p) d -> p t d", p=P)

    s.dma("sync", R[:].rearrange("p (t d) -> p t d", t=NT), r2r, sig="dld")
    s.dma("sync", ident_bf[:], ident, sig="dld")
    s.dma("sync", ones_st[:], ones_r, sig="dld")
    s.dma("sync", consts[:], consts_i, sig="dld")
    s.dma("sync", wstage[0:1, 0:D], b_out, sig="dld")
    s.wait("vector", "dld")
    s.op("vector", lambda e: e.tensor_copy(ones_bf[:], ones_st[:]))
    s.op("vector", lambda e: e.tensor_copy(bias_bf[:], wstage[0:1, 0:D]),
         sig="cast_done")
    for h2 in range(2):
        s.wait("sync", "cast_done", 1 + h2)
        s.dma("sync", wstage[:].rearrange("p (k o) -> p k o", k=4),
              w_out[512 * h2:512 * (h2 + 1), :].rearrange("(k p) o -> p k o", p=P),
              sig="wld")
        s.wait("vector", "wld")
        s.op("vector",
             lambda e, h2=h2: e.tensor_copy(
                 wo_bf[:, h2 * 4 * D:(h2 + 1) * 4 * D], wstage[:]),
             sig="cast_done")

    nd_c = 0
    rc_c = 0
    w_c = 0
    wv_c = 0
    for h in range(NH):
        s.wait("sync", "nd_mm", nd_c)
        s.wait("sync", "wv_mm", wv_c)
        for j in range(4):
            s.dma("sync", kq[0:64, j * TOK:(j + 1) * TOK],
                  rk4[j * D + h * HD: j * D + (h + 1) * HD, :], sig="hld")
            s.dma("sync", kq[64:128, j * TOK:(j + 1) * TOK],
                  k4[j * D + h * HD: j * D + (h + 1) * HD, :], sig="hld")
            s.dma("sync",
                  v_h[:, j * 16 * HD:(j + 1) * 16 * HD]
                  .rearrange("p (kt c) -> p kt c", kt=16),
                  v4[j * TOK:(j + 1) * TOK, h * HD:(h + 1) * HD]
                  .rearrange("(kt p) c -> p kt c", p=P), sig="hld")
        s.dma("sync", rq[0:64, :], rqf[h * HD:(h + 1) * HD, :], sig="hld")
        s.dma("sync", rq[64:128, :], qf[h * HD:(h + 1) * HD, :], sig="hld")
        s.wait("tensor", "hld", 16 * 14 * (h + 1))
        obase = 64 * (h % 2)
        for g in range(4):
            s.wait("tensor", "oev")
            for kt in range(NKT):
                pi = kt % 2
                if kt >= 2:
                    s.wait("tensor", "w_done", w_c - 1)
                    s.wait("tensor", "rc_done", rc_c - 1)
                s.op("tensor",
                     lambda e, pi=pi, kt=kt, g=g: e.matmul(
                         psN[pi][:], kq[0:64, kt * P:(kt + 1) * P],
                         rq[0:64, g * 512:(g + 1) * 512],
                         start=True, stop=True, tile_position=(0, 0)))
                s.op("tensor",
                     lambda e, pi=pi, kt=kt, g=g: e.matmul(
                         psD[pi][:], kq[64:128, kt * P:(kt + 1) * P],
                         rq[64:128, g * 512:(g + 1) * 512],
                         start=True, stop=True, tile_position=(64, 0)),
                     sig="nd_mm")
                nd_c += 1
                if kt >= 2:
                    s.wait("scalar", "w_done", w_c - 1)
                s.wait("scalar", "nd_mm", nd_c)
                s.op("scalar",
                     lambda e, pi=pi: p.act_recip(e, rc[pi][:], psD[pi][:]),
                     sig="rc_done")
                rc_c += 1
                if kt >= 2:
                    s.wait("vector", "wv_mm", wv_c - 1)
                s.wait("vector", "rc_done", rc_c)
                s.op("vector",
                     lambda e, pi=pi: e.tensor_mul(wT[pi][:], psN[pi][:],
                                                   rc[pi][:]),
                     sig="w_done")
                w_c += 1
                s.wait("tensor", "w_done", w_c)
                s.op("tensor",
                     lambda e, pi=pi, kt=kt, obase=obase: e.matmul(
                         psO[obase:obase + 64, :], v_h[:, kt * HD:(kt + 1) * HD],
                         wT[pi][:], start=(kt == 0), stop=(kt == NKT - 1),
                         tile_position=(0, obase), skip_group_check=True),
                     sig="wv_mm")
                wv_c += 1
            s.wait("vector", "wv_mm", wv_c)
            s.op("vector",
                 lambda e, h=h, g=g, obase=obase: e.tensor_copy(
                     attn_fm[obase:obase + 64,
                             (h // 2) * TOK + g * 512:(h // 2) * TOK + g * 512 + 512],
                     psO[obase:obase + 64, :]),
                 sig="oev")

    s.wait("tensor", "oev")
    s.wait("tensor", "w_done", w_c)
    s.wait("tensor", "rc_done", rc_c)
    for t in range(NT):
        R_t = R[:, t * D:(t + 1) * D]
        for half in range(2):
            pshalf = ptA if half == 0 else ptB
            if not (t == 0 and half == 0):
                s.wait("tensor", "r_done")
            for k in range(8):
                s.op("tensor",
                     lambda e, k=k, half=half, t=t, ps_=pshalf: e.matmul(
                         ps_[:], attn_fm[:, k * TOK + t * P:k * TOK + (t + 1) * P],
                         wo_bf[:, k * D + half * 512:k * D + half * 512 + 512],
                         start=(k == 0), stop=False))
            s.op("tensor",
                 lambda e, half=half, ps_=pshalf: e.matmul(
                     ps_[:], ones_bf[0:1, :],
                     bias_bf[0:1, half * 512:half * 512 + 512],
                     start=False, stop=True),
                 sig="o_mm")
            s.wait("vector", "o_mm")
            s.op("vector",
                 lambda e, half=half, Rt=R_t, ps_=pshalf: e.tensor_add(
                     Rt[:, half * 512:half * 512 + 512],
                     Rt[:, half * 512:half * 512 + 512], ps_[:]),
                 sig="r_done")
        s.wait("sync", "r_done")
        s.dma("sync", r3r[:, t, :], R_t, sig="rst")
    s.wait("gpsimd", "rst")
    return p.finish()


def run_l1a_attn(r2_l, qk_res, w_out, b_out):
    nc = build_l1a_attn()
    base = dict(w_out=np.ascontiguousarray(w_out), b_out=b_out.reshape(1, D),
                ident_bf=to_bf16(_hc["ident_bf"]), ones_row=_hc["ones_row"],
                consts=_hc["consts"])
    in_maps = []
    for c in range(NCORES):
        grp = (c // 4) * 4
        peers = [grp + j for j in range(4)]
        rk4 = np.concatenate([np.asarray(qk_res[j]["rk_fm"]) for j in peers], 0)
        k4 = np.concatenate([np.asarray(qk_res[j]["k_fm"]) for j in peers], 0)
        v4 = np.concatenate([np.asarray(qk_res[j]["v_tm"]) for j in peers], 0)
        in_maps.append(dict(base, r2=np.ascontiguousarray(r2_l[c]),
                            rqf=np.asarray(qk_res[c]["rq_fm"]),
                            qf=np.asarray(qk_res[c]["q_fm"]),
                            rk4=rk4, k4=k4, v4=v4))
    res = run_bass_kernel_spmd(nc, in_maps, core_ids=list(range(NCORES)))
    return res.results



NE1 = 8
DFF = 4096


def build_l1_ffn():
    p = Prog("l1_ffn")
    s = p.s
    r3 = p.inp("r3", [TOK, D])
    w1_i = p.inp("w1", [NE1 * D, DFF])
    b1_i = p.inp("b1", [NE1, DFF])
    w2_i = p.inp("w2", [NE1 * DFF, D])
    b2_i = p.inp("b2", [NE1, D])
    ident = p.inp("ident_bf", [P, P], BF16)
    ones_r = p.inp("ones_row", [1, P])
    ones5_i = p.inp("ones_512", [1, GTOK])
    ladd24_i = p.inp("ladd24", [P, 24])
    mod6_i = p.inp("mod6", [P, 6])
    eq9_i = p.inp("eq9", [P, 9])
    consts_i = p.inp("consts", [P, 8])

    o_r4 = p.out("r4", [TOK, D])
    route_dram = p.nc.dram_tensor("route_dram1", [NE1, TOK], BF16).ap()

    R = p.sb("R", [P, 2 * D])
    acc = p.sb("acc", [P, 8 * TOK], BF16)
    t_bf = p.sb("t_bf", [P, 8 * TOK], BF16)
    s_half = p.sb("s_half", [P, 16 * TOK], BF16)
    w1_bf = p.sb("w1_bf", [P, 8 * 2048], BF16)
    w2_bf = [p.sb("w2_bf0", [P, 16 * P], BF16)] * 2
    wstage = p.sb("wstage", [P, 2 * D])
    ident_bf = p.sb("identb", [P, P], BF16)
    ones_bf = p.sb("ones_bf", [1, P], BF16)
    ones_g_bf = p.sb("ones_g_bf", [1, GTOK], BF16)
    ones_st = p.sb("ones_st", [1, P])
    consts = p.sb("consts_s", [P, 8])
    ladd24 = p.sb("ladd24_s", [P, 24])
    mod6 = p.sb("mod6_s", [P, 6])
    eq9 = p.sb("eq9_s", [P, 9])
    b1_bf = p.sb("b1_bf", [1, DFF], BF16)
    b2_bf = p.sb("b2_bf", [NE1, D], BF16)
    route_fm = p.sb("route_fm", [NE1, TOK], BF16)
    route_row = p.sb("route_row", [1, TOK], BF16)
    bc_bf2 = [p.sb(f"bc_bf{i}", [P, GTOK], BF16) for i in range(NG)]
    t0 = p.sb("t0", [P, 1]); av = p.sb("av", [P, 1]); f24 = p.sb("f24", [P, 24])
    fv = p.sb("fv", [P, 1]); sgn = p.sb("sgn", [P, 1]); trn = p.sb("trn", [P, 1])
    uu = p.sb("uu", [P, 1]); m6 = p.sb("m6", [P, 6]); q6 = p.sb("q6", [P, 1])
    hh = p.sb("hh", [P, 1]); ge9 = p.sb("ge9", [P, 9]); i8 = p.sb("i8", [P, 8])
    route32 = p.sb("route32", [P, 8]); route_b = p.sb("route_b", [P, 8], BF16)
    scr = {k: p.sb("scr_" + k, [P, 1])
           for k in ["ss", "sum", "mean", "ms", "var", "std", "rstd"]}
    scr["sq"] = s_half[:, D:3 * D].bitcast(F32)
    y_tmp = p.sb("y_tmp", [P, 512])
    ln_bf = s_half[:, 0:D]
    ln4_t = wstage[:, 0:D]

    ptA = p.ps("ptA", [P, P], BF16)
    ptB = p.ps("ptB", [P, P], BF16)
    pm = p.ps("pm", [P, 512])
    po = p.ps("po", [P, 512])
    pbc = p.ps("pbc", [P, GTOK])

    r3r = r3.rearrange("(t p) d -> p t d", p=P)
    r4r = o_r4.rearrange("(t p) d -> p t d", p=P)

    s.dma("sync", ident_bf[:], ident, sig="dld")
    s.dma("sync", ones_st[:], ones_r, sig="dld")
    s.dma("sync", wstage[0:1, D:D + GTOK], ones5_i, sig="dld")
    s.dma("sync", consts[:], consts_i, sig="dld")
    s.dma("sync", ladd24[:], ladd24_i, sig="dld")
    s.dma("sync", mod6[:], mod6_i, sig="dld")
    s.dma("sync", eq9[:], eq9_i, sig="dld")
    s.dma("sync", wstage[0:NE1, 0:D], b2_i, sig="dld")
    s.dma("sync", R[:, 0:D], r3r[:, 0, :], sig="p1ld")
    s.dma("sync", R[:, D:2 * D], r3r[:, 1, :], sig="p1ld")
    s.wait("vector", "dld")
    s.op("vector", lambda e: e.tensor_copy(ones_bf[:], ones_st[:]))
    s.op("vector", lambda e: e.tensor_copy(ones_g_bf[:], wstage[0:1, D:D + GTOK]))
    s.op("vector", lambda e: e.tensor_copy(b2_bf[:], wstage[0:NE1, 0:D]),
         sig="init_cast")
    s.wait("scalar", "p1ld", 16)

    for t in range(NT):
        R_t = R[:, (t % 2) * D:(t % 2 + 1) * D]
        s.wait("vector", "p1ld", 16 * (t + 1))
        s.wait("scalar", "p1ld", 16 * (t + 1))
        emit_ln_tile(p, s, R_t, consts, scr, None, out_f32=ln4_t[:],
                     sig="ln_done")
        if t + 2 < NT:
            s.wait("sync", "ln_done", t + 1)
            s.dma("sync", R[:, (t % 2) * D:(t % 2 + 1) * D],
                  r3r[:, t + 2, :], sig="p1ld")
        s.opc("vector", lambda e: e.tensor_add(t0[:], ln4_t[:, 0:1],
                                               ln4_t[:, 1:2]))
        s.wait("vector", "hz_vector")
        s.op("vector", lambda e: e.tensor_copy(sgn[:], t0[:]), sig="rt_v2s")
        s.wait("scalar", "rt_v2s")
        s.op("scalar", lambda e: e.activation(av[:], t0[:], AF.Abs))
        s.op("scalar", lambda e: e.sign(sgn[:], t0[:],
                                        bias=consts[:, C_ZERO:C_ZERO + 1]),
             sig="rt_s2v")
        s.wait("vector", "rt_s2v")
        s.opc("vector", lambda e: e.tensor_tensor(
            f24[:], av[:, 0:1].broadcast_to([P, 24]), ladd24[:], ALU.is_ge))
        s.opc("vector", lambda e: e.reduce_sum(fv[:], f24[:], axis=AXX))
        s.opc("vector", lambda e: e.tensor_mul(trn[:], fv[:], sgn[:]))
        s.opc("vector", lambda e: e.tensor_scalar(uu[:], trn[:], 24.0, None,
                                                  ALU.add))
        s.opc("vector", lambda e: e.tensor_tensor(
            m6[:], uu[:, 0:1].broadcast_to([P, 6]), mod6[:], ALU.is_ge))
        s.opc("vector", lambda e: e.reduce_sum(q6[:], m6[:], axis=AXX))
        s.opc("vector", lambda e: e.scalar_tensor_tensor(
            hh[:], q6[:], -8.0, uu[:], ALU.mult, ALU.add))
        s.opc("vector", lambda e: e.tensor_tensor(
            ge9[:], hh[:, 0:1].broadcast_to([P, 9]), eq9[:], ALU.is_ge))
        s.opc("vector", lambda e: e.tensor_sub(i8[:], ge9[:, 0:8], ge9[:, 1:9]))
        s.opc("vector", lambda e: e.tensor_add(route32[:, 1:8], i8[:, 1:8],
                                               i8[:, 0:7]))
        s.opc("vector", lambda e: e.tensor_add(route32[:, 0:1], i8[:, 0:1],
                                               i8[:, 7:8]))
        s.opc("vector", lambda e: e.tensor_copy(route_b[:], route32[:]))
        s.wait("vector", "hz_vector")
        s.wait("vector", "tr_t2v")
        s.op("vector", lambda e: e.tensor_copy(ln_bf[:], ln4_t[:]),
             sig="route_done")
        s.wait("tensor", "route_done")
        s.wait("tensor", "tr_v2t")
        s.op("tensor", lambda e: e.transpose(ptA[0:8, 0:P], route_b[:],
                                             ident_bf[:]), sig="tr_t2v")
        s.wait("vector", "tr_t2v")
        s.op("vector",
             lambda e, t=t: e.tensor_copy(route_fm[0:8, t * P:(t + 1) * P],
                                          ptA[0:8, 0:P]),
             sig="tr_v2t")
        emit_transpose8(p, s, ln_bf, None, ident_bf, (ptA, ptB),
                        evac=lambda e, k, bank, t=t: e.tensor_copy(
                            t_bf[:, k * TOK + t * P: k * TOK + (t + 1) * P],
                            bank[:]))
    s.wait("sync", "tr_v2t")
    s.dma("sync", route_dram, route_fm[:], sig="route_st")

    nw1 = 0
    for e_i in range(NE1):
        s.wait("sync", "route_st", 16)
        if e_i > 0:
            s.wait("sync", "bc_mm", 4 * e_i)
        s.dma("sync", route_row[:], route_dram[e_i:e_i + 1, :], sig="route_ld")
        s.wait("tensor", "route_ld", 16 * (e_i + 1))
        for g in range(NG):
            s.wait("tensor", "bc_done")
            s.op("tensor",
                 lambda e, g=g: e.matmul(
                     pbc[:], ones_bf[0:1, :],
                     route_row[0:1, g * GTOK:(g + 1) * GTOK],
                     start=True, stop=True),
                 sig="bc_mm")
            s.wait("vector", "bc_mm")
            s.wait("vector", "y_evac", s.val("y_evac"))
            s.op("vector", lambda e, g=g: e.tensor_scalar(
                bc_bf2[g][:], pbc[:], 0.5, None, ALU.mult), sig="bc_done")
        s.wait("sync", "init_cast", 1)
        if e_i == 0:
            s.wait("sync", "route_st", 16)
            s.wait("sync", "tr_v2t")
        s.wait("sync", "w2cast", s.val("w2cast"))
        for bh in range(2):
            if bh == 1:
                s.wait("sync", "b1cast", s.val("b1cast"))
            s.dma("sync", wstage[0:1, 0:2048],
                  b1_i[e_i:e_i + 1, bh * 2048:(bh + 1) * 2048], sig="b1ld")
            s.wait("vector", "b1ld")
            s.wait("vector", "m_mm", s.val("m_mm"))
            s.op("vector",
                 lambda e, bh=bh: e.tensor_copy(
                     b1_bf[0:1, bh * 2048:(bh + 1) * 2048], wstage[0:1, 0:2048]),
                 sig="b1cast")
        for ihalf in range(2):
            for pc in range(8):
                rh, ch = divmod(pc, 2)
                if ihalf == 0 and pc == 0:
                    s.wait("sync", "b1cast", s.val("b1cast"))
                s.wait("sync", "w1cast", s.val("w1cast"))
                s.dma("sync", wstage[:].rearrange("p (k i) -> p k i", k=2),
                      w1_i[e_i * D + 256 * rh: e_i * D + 256 * (rh + 1),
                           ihalf * 2048 + ch * D: ihalf * 2048 + (ch + 1) * D]
                      .rearrange("(k p) i -> p k i", p=P), sig="w1ld")
                s.wait("gpsimd", "w1ld")
                s.wait("gpsimd", "m_mm", s.val("m_mm"))

                def cast_w1(e, rh=rh, ch=ch):
                    for kk in range(2):
                        e.tensor_copy(
                            w1_bf[:, (2 * rh + kk) * 2048 + ch * D:
                                  (2 * rh + kk) * 2048 + (ch + 1) * D],
                            wstage[:, kk * D:(kk + 1) * D])
                    return e.tensor_copy(w1_bf[:, 0:1], w1_bf[:, 0:1])
                s.op("gpsimd", cast_w1, sig="w1cast")
            s.wait("scalar", "y_mm", 64 * e_i + 32 * ihalf)
            for g in range(NG):
                if e_i == 0 and ihalf == 0:
                    s.wait("tensor", "tr_v2t")
                s.wait("tensor", "w1cast", s.val("w1cast"))
                s.wait("tensor", "b1cast", e_i + 1)
                for ich in range(16):
                    if not (e_i == 0 and ihalf == 0 and g == 0 and ich == 0):
                        s.wait("tensor", "m_evac")
                    for k in range(8):
                        s.op("tensor",
                             lambda e, k=k, ich=ich, g=g, ihalf=ihalf: e.matmul(
                                 pm[:],
                                 w1_bf[:, k * 2048 + ich * P: k * 2048 + ich * P + P],
                                 t_bf[:, k * TOK + g * GTOK: k * TOK + (g + 1) * GTOK],
                                 start=(k == 0), stop=False))
                    s.op("tensor",
                         lambda e, ich=ich, ihalf=ihalf: e.matmul(
                             pm[:],
                             b1_bf[0:1, ihalf * 2048 + ich * P:
                                   ihalf * 2048 + ich * P + P],
                             ones_g_bf[0:1, :], start=False, stop=True),
                         sig="m_mm")
                    s.wait("scalar", "m_mm")
                    s.op("scalar",
                         lambda e, ich=ich, g=g: e.activation(
                             s_half[:, ich * TOK + g * GTOK:
                                    ich * TOK + (g + 1) * GTOK],
                             pm[:], AF.Relu),
                         sig="m_evac")
            for och in range(8):
                wbuf = w2_bf[och % 2]
                s.wait("sync", "w2cast", s.val("w2cast"))
                s.dma("sync", wstage[:, 0:16 * P].rearrange("p (k o) -> p k o", k=16),
                      w2_i[e_i * DFF + ihalf * 2048: e_i * DFF + ihalf * 2048 + 2048,
                           och * P:(och + 1) * P]
                      .rearrange("(k p) o -> p k o", p=P), sig="w2ld")
                s.wait("gpsimd", "w2ld")
                s.wait("gpsimd", "y_mm", s.val("y_mm"))
                s.op("gpsimd",
                     lambda e, wbuf=wbuf: e.tensor_copy(wbuf[:], wstage[:, 0:16 * P]),
                     sig="w2cast")
                s.wait("tensor", "w2cast", s.val("w2cast"))
                for g in range(NG):
                    s.wait("tensor", "y_evac")
                    for k in range(16):
                        s.op("tensor",
                             lambda e, k=k, och=och, g=g, wbuf=wbuf: e.matmul(
                                 po[:], wbuf[:, k * P:(k + 1) * P],
                                 s_half[:, k * TOK + g * GTOK:
                                        k * TOK + (g + 1) * GTOK],
                                 start=(k == 0), stop=(k == 15)),
                             sig="y_mm" if k == 15 else None)
                    s.wait("vector", "y_mm")
                    dst = acc[:, och * TOK + g * GTOK: och * TOK + (g + 1) * GTOK]
                    if e_i == 0 and ihalf == 0:
                        s.op("vector",
                             lambda e, d=dst, g=g: e.tensor_mul(d, po[:], bc_bf2[g][:]),
                             sig="y_evac")
                    else:
                        s.op("vector",
                             lambda e, g=g: e.tensor_mul(y_tmp[:], po[:], bc_bf2[g][:]))
                        s.op("vector",
                             lambda e, d=dst: e.tensor_add(d, d, y_tmp[:]),
                             sig="y_evac")

    s.wait("tensor", "init_cast")
    for g in range(NG):
        for och in range(8):
            s.wait("tensor", "y_evac")
            s.op("tensor",
                 lambda e, g=g, och=och: e.matmul(
                     po[:], b2_bf[0:NE1, och * P:(och + 1) * P],
                     route_fm[0:NE1, g * GTOK:(g + 1) * GTOK],
                     start=True, stop=True),
                 sig="y_mm")
            s.wait("vector", "y_mm")
            dst = acc[:, och * TOK + g * GTOK: och * TOK + (g + 1) * GTOK]
            s.op("vector", lambda e, d=dst: e.tensor_add(d, d, po[:]),
                 sig="y_evac")

    s.dma("sync", R[:, 0:D], r3r[:, 0, :], sig="p3ld")
    s.dma("sync", R[:, D:2 * D], r3r[:, 1, :], sig="p3ld")
    for t in range(NT):
        R_t = R[:, (t % 2) * D:(t % 2 + 1) * D]
        s.wait("vector", "p3ld", 16 * (t + 1))
        g, ti = divmod(t, 4)
        if t == 0:
            s.wait("tensor", "y_evac")
        s.wait("tensor", "tr_v2t")
        for k in range(8):
            bank = (ptA, ptB)[k % 2]
            if k >= 2:
                s.wait("tensor", "tr_v2t")
            s.op("tensor",
                 lambda e, k=k, bank=bank, t=t: e.transpose(
                     bank[:], acc[:, k * TOK + t * P: k * TOK + (t + 1) * P],
                     ident_bf[:]),
                 sig="tr_t2v")
            s.wait("vector", "tr_t2v")
            s.op("vector",
                 lambda e, k=k, bank=bank, Rt=R_t: e.tensor_add(
                     Rt[:, k * P:(k + 1) * P], Rt[:, k * P:(k + 1) * P],
                     bank[:]),
                 sig="tr_v2t")
        s.wait("sync", "tr_v2t")
        s.dma("sync", r4r[:, t, :], R_t, sig="rst")
        if t + 2 < NT:
            s.wait("sync", "rst", 16 * (t + 1))
            s.dma("sync", R[:, (t % 2) * D:(t % 2 + 1) * D],
                  r3r[:, t + 2, :], sig="p3ld")

    s.wait("gpsimd", "rst")
    return p.finish()


def run_l1_ffn(r3_l, W1, b1, W2, b2):
    nc = build_l1_ffn()
    base = dict(w1=np.ascontiguousarray(W1).reshape(NE1 * D, DFF),
                b1=np.ascontiguousarray(b1),
                w2=np.ascontiguousarray(W2).reshape(NE1 * DFF, D),
                b2=np.ascontiguousarray(b2),
                ident_bf=to_bf16(_hc["ident_bf"]), ones_row=_hc["ones_row"],
                ones_512=_hc["ones_512"], ladd24=_hc["ladd24"],
                mod6=_hc["mod6"], eq9=_hc["eq9"], consts=_hc["consts"])
    in_maps = [dict(base, r3=np.ascontiguousarray(r3_l[c]))
               for c in range(NCORES)]
    res = run_bass_kernel_spmd(nc, in_maps, core_ids=list(range(NCORES)))
    return res.results


def rope_tables(core):
    q = core % 4
    half = 32
    freqs = (np.arange(half, dtype=np.float32) / np.float32(half)).astype(np.float32)
    theta = (1.0 / (10000.0 ** freqs.astype(np.float64))).astype(np.float32)
    pos = np.arange(q * TOK, (q + 1) * TOK, dtype=np.float32)
    ang = (pos[:, None] * theta[None, :]).astype(np.float32)
    cos = np.cos(ang.astype(np.float64)).astype(np.float32)
    sin = np.sin(ang.astype(np.float64)).astype(np.float32)
    return np.tile(cos, (1, 16)), np.tile(sin, (1, 16))


def run_l1a_qkv(r2_l, w_qkv, b_qkv):
    nc = build_l1a_qkv()
    base = dict(w_qkv=np.ascontiguousarray(w_qkv),
                b_qkv=b_qkv.reshape(1, 3 * D),
                ident_bf=to_bf16(_hc["ident_bf"]), ones_row=_hc["ones_row"],
                consts=_hc["consts"])
    in_maps = []
    for c in range(NCORES):
        cosr, sinr = rope_tables(c)
        in_maps.append(dict(base, r2=np.ascontiguousarray(r2_l[c]),
                            cosr=cosr, sinr=sinr))
    res = run_bass_kernel_spmd(nc, in_maps, core_ids=list(range(NCORES)))
    return res.results



def shard_tokens(a):
    out = []
    for c in range(NCORES):
        b, q = divmod(c, 4)
        out.append(np.ascontiguousarray(a[b, q * TOK:(q + 1) * TOK, :]))
    return out


def unshard_tokens(shards):
    a = np.empty((2, 8192, D), np.float32)
    for c, sh in enumerate(shards):
        b, q = divmod(c, 4)
        a[b, q * TOK:(q + 1) * TOK, :] = sh
    return a


_hc = host_consts()


def run_l0a_local(x_shards, Wk, bk, Wv, bv):
    nc = build_l0a_local()
    wkT = np.ascontiguousarray(Wk.T)
    wvT = np.ascontiguousarray(Wv.T)
    base = dict(wkT=wkT, wvT=wvT, bk=bk.reshape(1, D), bv=bv.reshape(1, D),
                ident_bf=to_bf16(_hc["ident_bf"]), ones_row=_hc["ones_row"],
                consts=_hc["consts"])
    in_maps = [dict(base, x=x_shards[c]) for c in range(NCORES)]
    res = run_bass_kernel_spmd(nc, in_maps, core_ids=list(range(NCORES)))
    return res.results


def kernel(**inputs):
    g = {k: np.ascontiguousarray(np.asarray(v, np.float32))
         for k, v in inputs.items()}
    x_shards = shard_tokens(g["L0_attn_x"])
    resA = run_l0a_local(x_shards, g["L0_attn_Wk"], g["L0_attn_bk"],
                         g["L0_attn_Wv"], g["L0_attn_bv"])
    kp_l = [np.asarray(r["kp"]) for r in resA]
    v_l = [np.asarray(r["v"]) for r in resA]
    sl_l = [np.asarray(r["s_local"]) for r in resA]
    ks_l = [np.asarray(r["kp_sum"]) for r in resA]
    resB = run_l0a_finish(x_shards, g["L0_attn_Wq"], g["L0_attn_bq"],
                          g["L0_attn_Wo"], g["L0_attn_bo"],
                          kp_l, v_l, sl_l, ks_l)
    r1_l = [np.ascontiguousarray(np.asarray(r["r1"])) for r in resB]
    resC = run_l0_ffn(r1_l, g["L0_ffn_scale"], g["L0_ffn_gate_w"],
                      g["L0_ffn_gate_b"], g["L0_ffn_mlp1_w"],
                      g["L0_ffn_mlp1_b"], g["L0_ffn_mlp2_w"],
                      g["L0_ffn_mlp2_b"])
    r2_l = [np.ascontiguousarray(np.asarray(r["r2"])) for r in resC]
    resD = run_l1a_qkv(r2_l, g["L1_attn_w_qkv"], g["L1_attn_b_qkv"])
    resE = run_l1a_attn(r2_l, resD, g["L1_attn_w_out"], g["L1_attn_b_out"])
    r3_l = [np.ascontiguousarray(np.asarray(r["r3"])) for r in resE]
    resF = run_l1_ffn(r3_l, g["L1_ffn_W1"], g["L1_ffn_b1"],
                      g["L1_ffn_W2"], g["L1_ffn_b2"])
    r4_l = [np.asarray(r["r4"]) for r in resF]
    return unshard_tokens(r4_l)


# version 11 (final)
# speedup vs baseline: 2.1108x; 2.1108x over previous
import numpy as np
from contextlib import ExitStack

import concourse.bass as bass
import concourse.mybir as mybir
from concourse.bass_utils import run_bass_kernel_spmd

F32 = mybir.dt.float32
BF16 = mybir.dt.bfloat16
AF = mybir.ActivationFunctionType
ALU = mybir.AluOpType
AXX = mybir.AxisListType.X

P = 128
D = 1024
TOK = 2048
NT = TOK // P
NCORES = 8
EPS_LN = 1e-5
EPS_DEN = 1e-6

ENGS = ("sync", "vector", "scalar", "tensor", "gpsimd")

C_EPS_LN = 0
C_ZERO = 1
C_EPS_DEN = 2
C_ONE = 3


class Sched:

    def __init__(self, nc, es):
        self.nc = nc
        self.es = es
        self.ops = {e: [] for e in ENGS}
        self._sems = {}
        self._cnt = {}

    def sem(self, name):
        if name not in self._sems:
            self._sems[name] = self.es.enter_context(self.nc.semaphore(name))
            self._cnt[name] = 0
        return name

    def op(self, eng, fn, sig=None, k=1):
        if sig is not None:
            self.sem(sig)
            sem = self._sems[sig]
            self._cnt[sig] += k

            def wrapped(e, fn=fn, sem=sem, k=k):
                fn(e).then_inc(sem, k)
            self.ops[eng].append(wrapped)
        else:
            self.ops[eng].append(fn)

    def dma(self, eng, out, in_, sig=None):
        self.op(eng, lambda e, o=out, i=in_: e.dma_start(out=o, in_=i), sig=sig, k=16)

    def wait(self, eng, sig, cnt=None):
        self.sem(sig)
        sem = self._sems[sig]
        want = self._cnt[sig] if cnt is None else cnt
        self.ops[eng].append(lambda e, sem=sem, want=want: e.wait_ge(sem, want))

    def opc(self, eng, fn):
        hz = "hz_" + eng
        self.wait(eng, hz)
        self.op(eng, fn, sig=hz)

    def val(self, sig):
        self.sem(sig)
        return self._cnt[sig]

    def build(self):
        with self.nc.Block() as block:
            for eng in ENGS:
                ops = self.ops[eng]
                if not ops:
                    continue

                def body(e, ops=ops):
                    for f in ops:
                        f(e)
                getattr(block, eng)(body)


class Prog:

    def __init__(self, name):
        self.nc = bass.Bass("TRN2", target_bir_lowering=False, name=name,
                            detect_race_conditions=False)
        self.es = ExitStack()
        self.s = Sched(self.nc, self.es)
        self._recip_names = []

    def inp(self, name, shape, dtype=F32):
        return self.nc.dram_tensor(name, list(shape), dtype, kind="ExternalInput").ap()

    def out(self, name, shape, dtype=F32):
        return self.nc.dram_tensor(name, list(shape), dtype, kind="ExternalOutput").ap()

    def sb(self, name, shape, dtype=F32):
        return self.es.enter_context(self.nc.sbuf_tensor(name, list(shape), dtype))

    def ps(self, name, shape, dtype=F32):
        return self.es.enter_context(self.nc.psum_tensor(name, list(shape), dtype))

    def act_recip(self, scalar, out, in_):
        i = scalar.activation(out, in_, AF.Sqrt)
        self._recip_names.append(i.ins.name)
        return i

    def finish(self):
        self.s.build()
        self.es.close()
        if self._recip_names:
            want = set(self._recip_names)
            n = 0
            for b in self.nc.main_func.blocks:
                insts = b.instructions
                for k, ins in enumerate(insts):
                    if type(ins).__name__ == "InstActivation" and ins.name in want:
                        insts[k] = ins.__replace__(func=AF.Reciprocal)
                        n += 1
            assert n == len(want), (n, len(want))
        return self.nc



def host_consts():
    c = {}
    c["ident_bf"] = np.eye(P, dtype=np.float32)
    c["triu_bf"] = np.triu(np.ones((P, P), np.float32))
    c["ones_row"] = np.ones((1, P), np.float32)
    c["ones_col"] = np.ones((P, 1), np.float32)
    c["ones_512"] = np.ones((1, 512), np.float32)
    c["ladd24"] = np.broadcast_to(np.arange(1, 25, dtype=np.float32), (P, 24)).copy()
    c["mod6"] = np.broadcast_to(np.array([8, 16, 24, 32, 40, 48], np.float32), (P, 6)).copy()
    c["eq9"] = np.broadcast_to(np.arange(9, dtype=np.float32), (P, 9)).copy()
    consts = np.zeros((P, 8), np.float32)
    consts[:, C_EPS_LN] = EPS_LN
    consts[:, C_ZERO] = 0.0
    consts[:, C_EPS_DEN] = EPS_DEN
    consts[:, C_ONE] = 1.0
    c["consts"] = consts
    return c


def to_bf16(a):
    import ml_dtypes
    return np.ascontiguousarray(a).astype(ml_dtypes.bfloat16)



def emit_ln_tile(p, s, R_t, consts, scr, out_bf, out_f32=None, rms_only=False,
                 extra_scale=None, sig=None):
    sq, ss, summ, mean, ms, var, std, rstd = (
        scr["sq"], scr["ss"], scr["sum"], scr["mean"], scr["ms"], scr["var"],
        scr["std"], scr["rstd"])
    s.op("scalar", lambda e: e.activation(sq[:], R_t, AF.Square), sig="ln_s2v")
    if not rms_only:
        s.opc("vector", lambda e: e.reduce_sum(summ[:], R_t, axis=AXX))
        s.opc("vector", lambda e: e.tensor_scalar(mean[:], summ[:], 1.0 / D,
                                                  None, ALU.mult))
        s.opc("vector", lambda e: e.tensor_mul(ms[:], mean[:], mean[:]))
    s.wait("vector", "ln_s2v")
    s.opc("vector", lambda e: e.reduce_sum(ss[:], sq[:], axis=AXX))
    s.wait("vector", "hz_vector")
    if rms_only:
        s.op("vector", lambda e: e.tensor_scalar(var[:], ss[:], 1.0 / D, None,
                                                 ALU.mult), sig="ln_v2s")
    else:
        s.op("vector", lambda e: e.scalar_tensor_tensor(var[:], ss[:], 1.0 / D,
                                                        ms[:], ALU.mult,
                                                        ALU.subtract),
             sig="ln_v2s")
    s.wait("scalar", "ln_v2s")
    s.op("scalar", lambda e: e.activation(std[:], var[:], AF.Sqrt,
                                          bias=consts[:, C_EPS_LN:C_EPS_LN + 1]),
         sig="ln_s2v")
    s.wait("vector", "ln_s2v")
    s.opc("vector", lambda e: e.reciprocal(rstd[:], std[:]))
    if extra_scale is not None:
        s.opc("vector", lambda e: e.tensor_mul(rstd[:], rstd[:], extra_scale))
    s.wait("vector", "hz_vector")
    if rms_only:
        if out_f32 is not None:
            s.op("vector", lambda e: e.tensor_scalar(out_f32, R_t, rstd[:, 0:1],
                                                     None, ALU.mult),
                 sig=(sig if out_bf is None else None))
            if out_bf is not None:
                s.op("vector", lambda e: e.tensor_copy(out_bf, out_f32), sig=sig)
        else:
            s.op("vector", lambda e: e.tensor_scalar(out_bf, R_t, rstd[:, 0:1],
                                                     None, ALU.mult), sig=sig)
    else:
        if out_f32 is not None:
            s.op("vector", lambda e: e.tensor_scalar(out_f32, R_t, mean[:, 0:1],
                                                     rstd[:, 0:1], ALU.subtract,
                                                     ALU.mult),
                 sig=(sig if out_bf is None else None))
            if out_bf is not None:
                s.op("vector", lambda e: e.tensor_copy(out_bf, out_f32), sig=sig)
        else:
            s.op("vector", lambda e: e.tensor_scalar(out_bf, R_t, mean[:, 0:1],
                                                     rstd[:, 0:1], ALU.subtract,
                                                     ALU.mult), sig=sig)


def emit_transpose8(p, s, src_bf, dst_fm, ident_bf, pt, src_cols=D,
                    ch="tr", evac=None):
    nk = src_cols // P
    t2v, v2t = ch + "_t2v", ch + "_v2t"
    s.wait("tensor", v2t)
    for k in range(nk):
        bank = pt[k % 2]
        if k >= 2:
            s.wait("tensor", v2t)
        s.op("tensor",
             lambda e, k=k, bank=bank: e.transpose(
                 bank[:], src_bf[:, k * P:(k + 1) * P], ident_bf[:]),
             sig=t2v)
        s.wait("vector", t2v)
        if evac is None:
            s.op("vector",
                 lambda e, k=k, bank=bank: e.tensor_copy(
                     dst_fm[:, k * P:(k + 1) * P], bank[:]),
                 sig=v2t)
        else:
            s.op("vector",
                 lambda e, k=k, bank=bank: evac(e, k, bank),
                 sig=v2t)


def emit_proj_half(p, s, lnfm, w_bf, bias_bf, ones_row_bf, psum, k_cols, half,
                   nk=8, sig="mm_t2x"):
    for k in range(nk):
        s.op("tensor",
             lambda e, k=k: e.matmul(
                 psum[:], lnfm[:, k * P:(k + 1) * P],
                 w_bf[:, k * k_cols + half * 512: k * k_cols + half * 512 + 512],
                 start=(k == 0), stop=False))
    s.op("tensor",
         lambda e: e.matmul(psum[:], ones_row_bf[0:1, :],
                            bias_bf[0:1, half * 512: half * 512 + 512],
                            start=False, stop=True),
         sig=sig)


def emit_phi_evac(p, s, psum, tmp1, tmp2, out_bf, out_f32=None):
    s.wait("scalar", "phi_v2s")
    s.op("scalar", lambda e: e.activation(tmp1[:], psum[:], AF.Relu))
    s.op("scalar", lambda e: e.activation(tmp2[:], psum[:], AF.Relu, scale=-1.0))
    s.op("scalar", lambda e: e.activation(tmp2[:], tmp2[:], AF.Exp, scale=-1.0),
         sig="phi_s2v")
    s.wait("vector", "phi_s2v")
    if out_f32 is not None:
        if out_bf is None:
            s.op("vector", lambda e: e.tensor_add(out_f32, tmp1[:], tmp2[:]),
                 sig="phi_v2s")
        else:
            s.op("vector", lambda e: e.tensor_add(out_f32, tmp1[:], tmp2[:]))
            s.op("vector", lambda e: e.tensor_copy(out_bf, out_f32),
                 sig="phi_v2s")
    else:
        s.op("vector", lambda e: e.tensor_add(out_bf, tmp1[:], tmp2[:]),
             sig="phi_v2s")



def build_l0a_local():
    p = Prog("l0a_local")
    s = p.s
    x = p.inp("x", [TOK, D])
    wkT = p.inp("wkT", [D, D])
    wvT = p.inp("wvT", [D, D])
    bk = p.inp("bk", [1, D])
    bv = p.inp("bv", [1, D])
    ident = p.inp("ident_bf", [P, P], BF16)
    ones_r = p.inp("ones_row", [1, P])
    consts_i = p.inp("consts", [P, 8])

    o_kp = p.out("kp", [TOK, D], BF16)
    o_v = p.out("v", [TOK, D], BF16)
    o_sl = p.out("s_local", [D, D])
    o_ks = p.out("kp_sum", [1, D])

    R = p.sb("R", [P, NT * D])
    wstage = p.sb("wstage", [P, 8 * D])
    wk_bf = p.sb("wk_bf", [P, 8 * D], BF16)
    wv_bf = p.sb("wv_bf", [P, 8 * D], BF16)
    bias_bf = p.sb("bias_bf", [1, 2 * D], BF16)
    bias_st = p.sb("bias_st", [1, 2 * D])
    ident_bf = p.sb("identb", [P, P], BF16)
    ones_bf = p.sb("ones_bf", [1, P], BF16)
    ones_col_bf = p.sb("ones_col_bf", [P, 1], BF16)
    ones_st = p.sb("ones_st", [1, P])
    consts = p.sb("consts_s", [P, 8])
    ln_bf = p.sb("ln_bf", [P, D], BF16)
    lnfm = p.sb("lnfm", [P, 8 * P], BF16)
    scr = {k: p.sb("scr_" + k, [P, 1])
           for k in ["ss", "sum", "mean", "ms", "var", "std", "rstd"]}
    scr["sq"] = p.sb("scr_sq", [P, D])
    tmp1 = p.sb("tmp1", [P, 512])
    tmp2 = p.sb("tmp2", [P, 512])
    kp_t = [p.sb(f"kp_t{i}", [P, D], BF16) for i in range(2)]
    v_t = [p.sb(f"v_t{i}", [P, D], BF16) for i in range(2)]
    S = p.sb("S", [P, 8 * D])
    kss = p.sb("kss", [1, D])

    ptA = p.ps("ptA", [P, P], BF16)
    ptB = p.ps("ptB", [P, P], BF16)
    pp0 = p.ps("pp0", [P, 512])
    pp1 = p.ps("pp1", [P, 512])
    pcs0 = p.ps("pcs0", [1, 512])
    pcs1 = p.ps("pcs1", [1, 512])

    xr = x.rearrange("(t p) d -> p t d", p=P)
    kpr = o_kp.rearrange("(t p) d -> p t d", p=P)
    vr = o_v.rearrange("(t p) d -> p t d", p=P)

    s.dma("sync", R[:].rearrange("p (t d) -> p t d", t=NT), xr, sig="dld")
    s.dma("sync", ident_bf[:], ident, sig="dld")
    s.dma("sync", ones_st[:], ones_r, sig="dld")
    s.dma("sync", consts[:], consts_i, sig="dld")
    s.dma("sync", bias_st[0:1, 0:D], bk, sig="dld")
    s.dma("sync", bias_st[0:1, D:2 * D], bv, sig="dld")
    s.dma("sync", wstage[:].rearrange("p (k o) -> p k o", k=8),
          wkT.rearrange("(k p) o -> p k o", p=P), sig="dld")
    s.wait("vector", "dld")
    s.op("vector", lambda e: e.tensor_copy(ones_bf[:], ones_st[:]))
    s.op("vector", lambda e: e.tensor_copy(ones_col_bf[:], consts[:, C_ONE:C_ONE + 1]))
    s.op("vector", lambda e: e.tensor_copy(bias_bf[:], bias_st[:]))
    s.op("vector", lambda e: e.tensor_copy(wk_bf[:], wstage[:]), sig="cast_done")
    s.wait("sync", "cast_done")
    s.dma("sync", wstage[:].rearrange("p (k o) -> p k o", k=8),
          wvT.rearrange("(k p) o -> p k o", p=P), sig="dld2")
    s.wait("vector", "dld2")
    s.op("vector", lambda e: e.tensor_copy(wv_bf[:], wstage[:]), sig="cast_done")
    s.wait("scalar", "dld")

    for t in range(NT):
        R_t = R[:, t * D:(t + 1) * D]
        kp_c = kp_t[t % 2]
        v_c = v_t[t % 2]
        if t >= 2:
            s.wait("vector", "dst", 16 * 2 * (t - 1))
        emit_ln_tile(p, s, R_t, consts, scr, ln_bf[:], sig="ln_done")
        s.wait("tensor", "ln_done")
        emit_transpose8(p, s, ln_bf, lnfm, ident_bf, (ptA, ptB))
        s.wait("tensor", "tr_v2t")
        if t == 0:
            s.wait("tensor", "cast_done", 2)
        s.wait("tensor", "kv_evac")
        for half in range(2):
            pshalf = pp0 if half == 0 else pp1
            emit_proj_half(p, s, lnfm, wk_bf, bias_bf, ones_bf, pshalf, D, half,
                           sig="kproj")
            s.wait("scalar", "kproj")
            emit_phi_evac(p, s, pshalf, tmp1, tmp2,
                          kp_c[:, half * 512:half * 512 + 512])
        s.op("vector", lambda e, b=kp_c: e.tensor_copy(b[:, 0:1], b[:, 0:1]),
             sig="kp_done")
        s.wait("tensor", "kp_done")
        for half in range(2):
            pshalf = pp0 if half == 0 else pp1
            emit_proj_half(p, s, lnfm, wv_bf, bias_bf[0:1, D:2 * D], ones_bf,
                           pshalf, D, half, sig="vproj")
            s.wait("vector", "vproj")
            s.op("vector",
                 lambda e, ps_=pshalf, half=half, b=v_c: e.tensor_copy(
                     b[:, half * 512:half * 512 + 512], ps_[:]),
                 sig="v_done")
        s.wait("tensor", "v_done")
        for k in range(8):
            for half in range(2):
                pshalf = pp0 if half == 0 else pp1
                s.op("tensor",
                     lambda e, k=k, ps_=pshalf, half=half, kp_c=kp_c, v_c=v_c:
                     e.matmul(ps_[:], kp_c[:, k * P:(k + 1) * P],
                              v_c[:, half * 512:half * 512 + 512],
                              start=True, stop=True),
                     sig="kv_mm")
            s.wait("vector", "kv_mm")
            for half in range(2):
                pshalf = pp0 if half == 0 else pp1
                dst = S[:, k * D + half * 512: k * D + half * 512 + 512]
                if t == 0:
                    s.op("vector",
                         lambda e, d=dst, ps_=pshalf: e.tensor_copy(d, ps_[:]),
                         sig="kv_evac")
                else:
                    s.op("vector",
                         lambda e, d=dst, ps_=pshalf: e.tensor_add(d, d, ps_[:]),
                         sig="kv_evac")
            if k < 7:
                s.wait("tensor", "kv_evac")
        for half in range(2):
            pcs = pcs0 if half == 0 else pcs1
            s.op("tensor",
                 lambda e, pcs=pcs, half=half, kp_c=kp_c: e.matmul(
                     pcs[:], ones_col_bf[:, 0:1],
                     kp_c[:, half * 512:half * 512 + 512],
                     start=(t == 0), stop=(t == NT - 1)),
                 sig="cs_mm" if t == NT - 1 else None)
        s.wait("sync", "kp_done", s.val("kp_done"))
        s.wait("sync", "v_done")
        s.dma("sync", kpr[:, t, :], kp_c[:], sig="dst")
        s.dma("sync", vr[:, t, :], v_c[:], sig="dst")

    s.wait("vector", "cs_mm")
    s.op("vector", lambda e: e.tensor_copy(kss[0:1, 0:512], pcs0[:]))
    s.op("vector", lambda e: e.tensor_copy(kss[0:1, 512:1024], pcs1[:]),
         sig="kss_done")
    s.wait("sync", "kss_done")
    s.dma("sync", o_ks, kss[:], sig="dst")
    s.wait("sync", "kv_evac")
    s.dma("sync", o_sl.rearrange("(k p) e -> p k e", p=P),
          S[:].rearrange("p (k e) -> p k e", k=8), sig="dst")
    s.wait("gpsimd", "dst")

    return p.finish()



def build_l0a_finish():
    p = Prog("l0a_finish")
    s = p.s
    x = p.inp("x", [TOK, D])
    wqT = p.inp("wqT", [D, D])
    woT = p.inp("woT", [D, D])
    bq = p.inp("bq", [1, D])
    bo = p.inp("bo", [1, D])
    kp_in = p.inp("kp", [TOK, D], BF16)
    v_in = p.inp("v", [TOK, D], BF16)
    s_starts = p.inp("s_starts", [3 * D, D])
    kp_starts = p.inp("kp_starts", [1, 3 * D])
    ident = p.inp("ident_bf", [P, P], BF16)
    triu_i = p.inp("triu", [P, P])
    ones_r = p.inp("ones_row", [1, P])
    consts_i = p.inp("consts", [P, 8])

    o_r1 = p.out("r1", [TOK, D])

    R = p.sb("R", [P, NT * D])
    wstage = p.sb("wstage", [P, 4 * D])
    wq_bf = p.sb("wq_bf", [P, 8 * D], BF16)
    wo_bf = p.sb("wo_bf", [P, 8 * D], BF16)
    bias_bf = p.sb("bias_bf", [1, 2 * D], BF16)
    ident_bf = p.sb("identb", [P, P], BF16)
    triu_f = p.sb("triu_f", [P, P])
    triu_bf = p.sb("triu_b", [P, P], BF16)
    ones_bf = p.sb("ones_bf", [1, P], BF16)
    ones_col_bf = p.sb("ones_col_bf", [P, 1], BF16)
    ones_st = p.sb("ones_st", [1, P])
    consts = p.sb("consts_s", [P, 8])
    S = p.sb("S", [P, 8 * D])
    S_bf = p.sb("S_bf", [P, 8 * D], BF16)
    z_run = p.sb("z_run", [1, D])
    z_bf = p.sb("z_bf", [1, D], BF16)
    ln_bf = p.sb("ln_bf", [P, D], BF16)
    lnfm = p.sb("lnfm", [P, 8 * P], BF16)
    qp_f32 = p.sb("qp_f32", [P, D])
    qp_bf = p.sb("qp_bf", [P, D], BF16)
    qpfm = p.sb("qpfm", [P, 8 * P], BF16)
    kpfm = p.sb("kpfm", [P, 8 * P], BF16)
    kp_t = [p.sb(f"kp_t{i}", [P, D], BF16) for i in range(2)]
    v_t = [p.sb(f"v_t{i}", [P, D], BF16) for i in range(2)]
    at_bf = p.sb("at_bf", [P, P], BF16)
    o_bf = p.sb("o_bf", [P, D], BF16)
    ofm = p.sb("ofm", [P, 8 * P], BF16)
    tmp1 = p.sb("tmp1", [P, 512])
    tmp2 = p.sb("tmp2", [P, 512])
    scr = {k: p.sb("scr_" + k, [P, 1])
           for k in ["ss", "sum", "mean", "ms", "var", "std", "rstd"]}
    scr["sq"] = p.sb("scr_sq", [P, D])
    dump = scr["sq"]
    den_h = [p.sb(f"den{i}", [P, 1]) for i in range(2)]
    den = p.sb("den", [P, 1])
    rden = p.sb("rden", [P, 1])

    ptA = p.ps("ptA", [P, P], BF16)
    ptB = p.ps("ptB", [P, P], BF16)
    pz = p.ps("pz", [P, 512])
    pcs = p.ps("pcs", [1, 512])
    pa0 = p.ps("pa0", [P, 512])
    pa1 = p.ps("pa1", [P, 512])
    pb = p.ps("pb", [P, P])

    xr = x.rearrange("(t p) d -> p t d", p=P)
    kpr = kp_in.rearrange("(t p) d -> p t d", p=P)
    vr = v_in.rearrange("(t p) d -> p t d", p=P)
    r1r = o_r1.rearrange("(t p) d -> p t d", p=P)

    s.dma("sync", R[:].rearrange("p (t d) -> p t d", t=NT), xr, sig="dld")
    s.dma("sync", ident_bf[:], ident, sig="dld")
    s.dma("sync", triu_f[:], triu_i, sig="dld")
    s.dma("sync", ones_st[:], ones_r, sig="dld")
    s.dma("sync", consts[:], consts_i, sig="dld")
    s.dma("sync", wstage[0:1, 0:D], bq, sig="dld")
    s.dma("sync", wstage[0:1, D:2 * D], bo, sig="dld")
    s.dma("sync", wstage[0:1, 2 * D:3 * D], kp_starts[0:1, 0:D], sig="dld")
    s.dma("sync", S[:].rearrange("p (k e) -> p k e", k=8),
          s_starts[0:D, :].rearrange("(k p) e -> p k e", p=P), sig="dld")
    s.wait("vector", "dld")
    s.op("vector", lambda e: e.tensor_copy(ones_bf[:], ones_st[:]))
    s.op("vector", lambda e: e.tensor_copy(ones_col_bf[:], consts[:, C_ONE:C_ONE + 1]))
    s.op("vector", lambda e: e.tensor_copy(bias_bf[:], wstage[0:1, 0:2 * D]))
    s.op("vector", lambda e: e.tensor_copy(z_run[:], wstage[0:1, 2 * D:3 * D]))
    s.op("vector", lambda e: e.tensor_copy(triu_bf[:], triu_f[:]), sig="cast_done")
    s.wait("sync", "cast_done", 1)
    s.dma("sync", wstage[0:1, 0:2 * D], kp_starts[0:1, D:3 * D], sig="kld")
    s.wait("vector", "kld")
    s.op("vector", lambda e: e.tensor_add(z_run[:], z_run[:],
                                          wstage[0:1, 0:D]))
    s.op("vector", lambda e: e.tensor_add(z_run[:], z_run[:],
                                          wstage[0:1, D:2 * D]))
    s.op("vector", lambda e: e.tensor_copy(z_bf[:], z_run[:]), sig="z_done")
    s.wait("sync", "z_done", 1)
    for wi, (wsrc, wdst) in enumerate([(wqT, wq_bf), (woT, wo_bf)]):
        for h in range(2):
            if (wi, h) != (0, 0):
                s.wait("sync", "cast_done", 1 + 2 * wi + h)
            s.dma("sync", wstage[:].rearrange("p (k o) -> p k o", k=4),
                  wsrc[512 * h:512 * (h + 1), :].rearrange("(k p) o -> p k o", p=P),
                  sig="wld")
            s.wait("vector", "wld")
            s.op("vector",
                 lambda e, wdst=wdst, h=h: e.tensor_copy(
                     wdst[:, h * 4 * D:(h + 1) * 4 * D], wstage[:]),
                 sig="cast_done")
    for j in (1, 2):
        for h in range(2):
            s.wait("sync", "cast_done" if (j, h) == (1, 0) else "s_add",
                   5 if (j, h) == (1, 0) else None)
            s.dma("sync", wstage[:].rearrange("p (k e) -> p k e", k=4),
                  s_starts[j * D + 512 * h: j * D + 512 * (h + 1), :]
                  .rearrange("(k p) e -> p k e", p=P), sig="sld")
            s.wait("vector", "sld")
            s.op("vector",
                 lambda e, h=h: e.tensor_add(
                     S[:, h * 4 * D:(h + 1) * 4 * D],
                     S[:, h * 4 * D:(h + 1) * 4 * D], wstage[:]),
                 sig="s_add")
    s.wait("gpsimd", "s_add", 4)
    s.op("gpsimd", lambda e: e.tensor_copy(S_bf[:], S[:]), sig="sbf_done")
    s.wait("scalar", "dld")

    for t0 in range(min(2, NT)):
        s.dma("sync", kp_t[t0][:], kpr[:, t0, :], sig="kpv_ld")
        s.dma("sync", v_t[t0][:], vr[:, t0, :], sig="kpv_ld")

    for t in range(NT):
        R_t = R[:, t * D:(t + 1) * D]
        kp_c = kp_t[t % 2]
        v_c = v_t[t % 2]
        emit_ln_tile(p, s, R_t, consts, scr, ln_bf[:], sig="ln_done")
        s.wait("tensor", "ln_done")
        emit_transpose8(p, s, ln_bf, lnfm, ident_bf, (ptA, ptB))
        s.wait("tensor", "tr_v2t")
        if t == 0:
            s.wait("tensor", "cast_done", 5)
        s.wait("tensor", "kv_evac")
        for half in range(2):
            pshalf = pa0 if half == 0 else pa1
            emit_proj_half(p, s, lnfm, wq_bf, bias_bf, ones_bf, pshalf, D, half,
                           sig="qproj")
            s.wait("scalar", "qproj")
            emit_phi_evac(p, s, pshalf, tmp1, tmp2,
                          qp_bf[:, half * 512:half * 512 + 512],
                          out_f32=qp_f32[:, half * 512:half * 512 + 512])
        s.op("vector", lambda e: e.tensor_copy(qp_bf[:, 0:1], qp_bf[:, 0:1]),
             sig="qp_done")
        s.wait("tensor", "qp_done")
        emit_transpose8(p, s, qp_bf, qpfm, ident_bf, (ptA, ptB))
        s.wait("tensor", "kpv_ld", 32 * (t + 1))
        s.wait("vector", "kpv_ld", 32 * (t + 1))
        emit_transpose8(p, s, kp_c, kpfm, ident_bf, (ptA, ptB))
        s.wait("tensor", "z_done")
        for half in range(2):
            s.op("tensor",
                 lambda e, half=half, b=kp_c: e.matmul(
                     pz[:], triu_bf[:],
                     b[:, half * 512:half * 512 + 512],
                     start=True, stop=False))
            s.op("tensor",
                 lambda e, half=half: e.matmul(
                     pz[:], ones_bf[0:1, :], z_bf[0:1, half * 512:half * 512 + 512],
                     start=False, stop=True),
                 sig="z_mm")
            s.wait("vector", "z_mm")
            s.op("vector",
                 lambda e, half=half, dh=den_h[half]: e.scalar_tensor_tensor(
                     dump[:, 0:512], pz[:], 1.0,
                     qp_f32[:, half * 512:half * 512 + 512],
                     ALU.mult, ALU.mult, accum_out=dh[:, 0:1]),
                 sig="den_acc")
            s.wait("tensor", "den_acc")
        s.wait("vector", "den_acc")
        s.opc("vector", lambda e: e.tensor_add(den[:], den_h[0][:], den_h[1][:]))
        s.opc("vector", lambda e: e.tensor_scalar(den[:], den[:], EPS_DEN, None,
                                                  ALU.add))
        s.opc("vector", lambda e: e.reciprocal(rden[:], den[:]))
        s.wait("vector", "hz_vector")
        for half in range(2):
            s.op("tensor",
                 lambda e, half=half, b=kp_c: e.matmul(
                     pcs[:], ones_col_bf[:, 0:1],
                     b[:, half * 512:half * 512 + 512],
                     start=True, stop=True),
                 sig="zcs_mm")
            s.wait("vector", "zcs_mm")
            s.op("vector",
                 lambda e, half=half: e.tensor_add(
                     z_run[0:1, half * 512:half * 512 + 512],
                     z_run[0:1, half * 512:half * 512 + 512], pcs[:]),
                 sig="zcs_evac")
            s.wait("tensor", "zcs_evac")
        s.op("vector", lambda e: e.tensor_copy(z_bf[:], z_run[:]), sig="z_done")
        s.wait("tensor", "sbf_done", 1 + 8 * t)
        for half in range(2):
            pshalf = pa0 if half == 0 else pa1
            for k in range(8):
                s.op("tensor",
                     lambda e, k=k, half=half, ps_=pshalf: e.matmul(
                         ps_[:], qpfm[:, k * P:(k + 1) * P],
                         S_bf[:, k * D + half * 512:k * D + half * 512 + 512],
                         start=(k == 0), stop=False))
        for k in range(8):
            s.op("tensor",
                 lambda e, k=k: e.matmul(
                     pb[:], kpfm[:, k * P:(k + 1) * P],
                     qpfm[:, k * P:(k + 1) * P],
                     start=(k == 0), stop=(k == 7)),
                 sig="at_mm" if k == 7 else None)
        s.wait("vector", "at_mm")
        s.op("vector", lambda e: e.tensor_mul(at_bf[:], pb[:], triu_f[:]),
             sig="at_evac")
        s.wait("tensor", "at_evac")
        for half in range(2):
            pshalf = pa0 if half == 0 else pa1
            s.op("tensor",
                 lambda e, half=half, ps_=pshalf, b=v_c: e.matmul(
                     ps_[:], at_bf[:],
                     b[:, half * 512:half * 512 + 512],
                     start=False, stop=True),
                 sig="num_mm")
        s.wait("vector", "num_mm")
        for half in range(2):
            pshalf = pa0 if half == 0 else pa1
            s.op("vector",
                 lambda e, half=half, ps_=pshalf: e.tensor_scalar(
                     o_bf[:, half * 512:half * 512 + 512], ps_[:], rden[:, 0:1],
                     None, ALU.mult),
                 sig="o_done")
        s.wait("tensor", "o_done")
        emit_transpose8(p, s, o_bf, ofm, ident_bf, (ptA, ptB))
        for half in range(2):
            pshalf = pa0 if half == 0 else pa1
            emit_proj_half(p, s, ofm, wo_bf, bias_bf[0:1, D:2 * D], ones_bf,
                           pshalf, D, half, sig="oproj")
            s.wait("vector", "oproj")
            s.op("vector",
                 lambda e, half=half, ps_=pshalf, Rt=R_t: e.tensor_add(
                     Rt[:, half * 512:half * 512 + 512],
                     Rt[:, half * 512:half * 512 + 512], ps_[:]),
                 sig="r_done")
        s.wait("tensor", "r_done")
        for k in range(8):
            for half in range(2):
                pshalf = pa0 if half == 0 else pa1
                s.op("tensor",
                     lambda e, k=k, ps_=pshalf, half=half, kp_b=kp_c, v_b=v_c:
                     e.matmul(ps_[:], kp_b[:, k * P:(k + 1) * P],
                              v_b[:, half * 512:half * 512 + 512],
                              start=True, stop=True),
                     sig="kv_mm")
            s.wait("vector", "kv_mm")
            for half in range(2):
                pshalf = pa0 if half == 0 else pa1
                dst = S[:, k * D + half * 512: k * D + half * 512 + 512]
                s.op("vector",
                     lambda e, d=dst, ps_=pshalf: e.tensor_add(d, d, ps_[:]),
                     sig="kv_evac")
            if k < 7:
                s.wait("tensor", "kv_evac")
            if t < NT - 1:
                s.wait("gpsimd", "kv_evac", s.val("kv_evac"))
                s.op("gpsimd",
                     lambda e, k=k: e.tensor_copy(
                         S_bf[:, k * D:(k + 1) * D], S[:, k * D:(k + 1) * D]),
                     sig="sbf_done")
        s.wait("sync", "r_done")
        s.dma("sync", r1r[:, t, :], R_t, sig="rst")
        if t + 2 < NT:
            s.wait("sync", "kv_mm")
            s.dma("sync", kp_t[(t + 2) % 2][:], kpr[:, t + 2, :], sig="kpv_ld")
            s.dma("sync", v_t[(t + 2) % 2][:], vr[:, t + 2, :], sig="kpv_ld")

    s.wait("gpsimd", "rst")
    return p.finish()


def run_l0a_finish(x_shards, Wq, bq, Wo, bo, kp_l, v_l, sl_l, ks_l):
    nc = build_l0a_finish()
    wqT = np.ascontiguousarray(Wq.T)
    woT = np.ascontiguousarray(Wo.T)
    base = dict(wqT=wqT, woT=woT, bq=bq.reshape(1, D), bo=bo.reshape(1, D),
                ident_bf=to_bf16(_hc["ident_bf"]), triu=_hc["triu_bf"],
                ones_row=_hc["ones_row"], consts=_hc["consts"])
    zeros_s = np.zeros((D, D), np.float32)
    zeros_k = np.zeros((1, D), np.float32)
    in_maps = []
    for c in range(NCORES):
        q = c % 4
        grp = (c // 4) * 4
        preds = [grp + j for j in range(q)]
        sst = [np.asarray(sl_l[j], np.float32) for j in preds]
        kst = [np.asarray(ks_l[j], np.float32).reshape(1, D) for j in preds]
        while len(sst) < 3:
            sst.append(zeros_s)
            kst.append(zeros_k)
        in_maps.append(dict(base, x=x_shards[c], kp=kp_l[c], v=v_l[c],
                            s_starts=np.concatenate(sst, 0),
                            kp_starts=np.concatenate(kst, 1)))
    res = run_bass_kernel_spmd(nc, in_maps, core_ids=list(range(NCORES)))
    return res.results



DBG_L0F = False
NE0 = 16
NG = 4
GTOK = 512
SW_SCALE = 8.0 / 1.702


def build_l0_ffn():
    p = Prog("l0_ffn")
    s = p.s
    r1 = p.inp("r1", [TOK, D])
    scale_fm_i = p.inp("scale_fm", [P, 8])
    gate_wT = p.inp("gate_wT", [D, 16])
    gate_b = p.inp("gate_b", [1, 16])
    w1p = p.inp("w1p", [NE0 * D, D])
    b1p = p.inp("b1p", [NE0, D])
    w2p = p.inp("w2p", [NE0 * D, D])
    b2_i = p.inp("b2", [NE0, D])
    ident = p.inp("ident_bf", [P, P], BF16)
    ident_f = p.inp("ident_f", [P, P])
    ones_r = p.inp("ones_row", [1, P])
    ones5_i = p.inp("ones_512", [1, GTOK])
    consts_i = p.inp("consts", [P, 8])

    o_r2 = p.out("r2", [TOK, D])
    route_dram = p.nc.dram_tensor("route_dram", [16, TOK], BF16).ap()

    R = p.sb("R", [P, 2 * D])
    acc = p.sb("acc", [P, 8 * TOK])
    t_bf = p.sb("t_bf", [P, 8 * TOK], BF16)
    wstage = p.sb("wstage", [P, 2 * D])
    w2stage = p.sb("w2stage", [P, 2 * D])
    w1_bf = p.sb("w1_bf", [P, 8 * D], BF16)
    w2_bf = p.sb("w2_bf", [P, 8 * D], BF16)
    tg32 = None
    s_fm = p.sb("s_fm", [P, 8 * GTOK], BF16)
    ident_bf = p.sb("identb", [P, P], BF16)
    ident_f32 = p.sb("identf", [P, P])
    ones_bf = p.sb("ones_bf", [1, P], BF16)
    ones_g_bf = p.sb("ones_g_bf", [1, GTOK], BF16)
    ones_g_st = p.sb("ones_g_st", [1, GTOK])
    ones_st = p.sb("ones_st", [1, P])
    consts = p.sb("consts_s", [P, 8])
    scale_fm = p.sb("scale_fm_s", [P, 8])
    gwT = p.sb("gwT", [P, 8 * 16])
    gb = p.sb("gb", [1, 16])
    b2_bf = p.sb("b2_bf", [16, D], BF16)
    b2_st = p.sb("b2_st", [16, D])
    b1_bf = p.sb("b1_bf", [1, D], BF16)
    route_fm = p.sb("route_fm", [16, TOK], BF16)
    route_row = p.sb("route_row", [1, TOK], BF16)
    bc_bf = p.sb("bc_bf", [P, GTOK], BF16)
    g_sb = p.sb("g_sb", [P, 16])
    g_w = [p.sb(f"g_w{i}", [P, 16]) for i in range(2)]
    gd = p.sb("gd", [P, 16])
    gmask = p.sb("gmask", [P, 16])
    ge_exp = p.sb("ge_exp", [P, 16])
    gem = p.sb("gem", [P, 16])
    route32 = p.sb("route32", [P, 16])
    route_b = p.sb("route_b", [P, 16], BF16)
    m1 = p.sb("m1", [P, 1]); m4 = p.sb("m4", [P, 1]); mi = p.sb("mi", [P, 1])
    negm1 = p.sb("negm1", [P, 1])
    ssum = p.sb("ssum", [P, 1]); rs = p.sb("rs", [P, 1])
    t_tm = p.sb("t_tm", [P, D])
    ln2_t = p.sb("ln2_t", [P, D])
    m_sb = p.sb("m_sb", [P, 512])
    sg_bf = p.sb("sg_bf", [P, 512], BF16)
    y_tmp = p.sb("y_tmp", [P, 512])
    scr = {k: p.sb("scr_" + k, [P, 1])
           for k in ["ss", "sum", "mean", "ms", "var", "std", "rstd"]}
    scr["sq"] = p.sb("scr_sq", [P, D])

    tg32 = acc
    ptA = p.ps("ptA", [P, P], BF16)
    ptB = p.ps("ptB", [P, P], BF16)
    ptC = p.ps("ptC", [P, P])
    ptD = p.ps("ptD", [P, P])
    pm = p.ps("pm", [P, 512])
    po = p.ps("po", [P, 512])
    pg = p.ps("pg", [P, 16])
    pbc = p.ps("pbc", [P, GTOK])

    r1r = r1.rearrange("(t p) d -> p t d", p=P)
    r2r = o_r2.rearrange("(t p) d -> p t d", p=P)
    zbc16 = consts[:, C_ZERO:C_ZERO + 1].broadcast_to([P, 16])

    s.dma("sync", ident_bf[:], ident, sig="dld")
    s.dma("sync", ident_f32[:], ident_f, sig="dld")
    s.dma("sync", ones_st[:], ones_r, sig="dld")
    s.dma("sync", ones_g_st[:], ones5_i, sig="dld")
    s.dma("sync", consts[:], consts_i, sig="dld")
    s.dma("sync", scale_fm[:], scale_fm_i, sig="dld")
    s.dma("sync", gwT[:].rearrange("p (k e) -> p k e", k=8),
          gate_wT.rearrange("(k p) e -> p k e", p=P), sig="dld")
    s.dma("sync", gb[:], gate_b, sig="dld")
    s.dma("sync", b2_st[:], b2_i, sig="dld")
    s.dma("sync", R[:, 0:D], r1r[:, 0, :], sig="p1ld")
    s.dma("sync", R[:, D:2 * D], r1r[:, 1, :], sig="p1ld")
    s.wait("vector", "dld")
    s.op("vector", lambda e: e.tensor_copy(ones_bf[:], ones_st[:]))
    s.op("vector", lambda e: e.tensor_copy(ones_g_bf[:], ones_g_st[:]))
    s.op("vector", lambda e: e.tensor_copy(b2_bf[:], b2_st[:]), sig="init_cast")
    s.wait("scalar", "dld")

    for g in range(NG):
        for ti in range(4):
            t = g * 4 + ti
            R_t = R[:, (t % 2) * D:(t % 2 + 1) * D]
            s.wait("vector", "p1ld", 16 * (t + 1))
            s.wait("scalar", "p1ld", 16 * (t + 1))
            emit_ln_tile(p, s, R_t, consts, scr, None, out_f32=ln2_t[:],
                         sig="ln_done")
            if t + 2 < NT:
                s.wait("sync", "ln_done", t + 1)
                s.dma("sync", R[:, (t % 2) * D:(t % 2 + 1) * D],
                      r1r[:, t + 2, :], sig="p1ld")
            s.wait("scalar", "ln_done")
            emit_ln_tile(p, s, ln2_t[:], consts, scr, None, out_f32=t_tm[:],
                         rms_only=True, sig="rms_done")
            s.wait("tensor", "rms_done")

            def evac_scale(e, k, bank, ti=ti):
                return e.tensor_scalar(
                    tg32[:, k * GTOK + ti * P: k * GTOK + ti * P + P],
                    bank[:], scale_fm[:, k:k + 1], None, ALU.mult)
            emit_transpose8(p, s, t_tm, None, ident_f32, (ptC, ptD), ch="trf",
                            evac=evac_scale)
            s.wait("vector", "trf_v2t")
            for k in range(8):
                s.op("vector",
                     lambda e, k=k, g=g, ti=ti: e.tensor_copy(
                         t_bf[:, k * TOK + g * GTOK + ti * P:
                              k * TOK + g * GTOK + ti * P + P],
                         tg32[:, k * GTOK + ti * P: k * GTOK + ti * P + P]),
                     sig="tbf_done")
            s.wait("tensor", "tbf_done")
            for k in range(8):
                s.op("tensor",
                     lambda e, k=k, ti=ti: e.matmul(
                         pg[:], tg32[:, k * GTOK + ti * P: k * GTOK + ti * P + P],
                         gwT[:, k * 16:(k + 1) * 16], start=(k == 0), stop=False))
            s.op("tensor",
                 lambda e: e.matmul(pg[:], ones_st[0:1, :], gb[0:1, :],
                                    start=False, stop=True), sig="g_mm")
            s.wait("vector", "g_mm")
            s.opc("vector", lambda e: e.tensor_copy(g_sb[:], pg[:]))
            s.wait("tensor", "hz_vector")
            cur = g_sb
            for it in range(4):
                dst_m = m1 if it == 0 else (m4 if it == 3 else mi)
                s.opc("vector", lambda e, c=cur, dm=dst_m: e.reduce_max(
                    dm[:], c[:], axis=AXX))
                if it < 3:
                    nxt = g_w[it % 2]
                    s.opc("vector", lambda e, c=cur, dm=dst_m: e.tensor_scalar(
                        gd[:], c[:], dm[:, 0:1], None, ALU.subtract))
                    s.opc("vector", lambda e: e.tensor_tensor(
                        gmask[:], gd[:], zbc16, ALU.is_ge))
                    s.opc("vector", lambda e, c=cur, n=nxt: e.scalar_tensor_tensor(
                        n[:], gmask[:], -1e9, c[:], ALU.mult, ALU.add))
                    cur = nxt
            s.opc("vector", lambda e: e.tensor_scalar(gd[:], g_sb[:], m4[:, 0:1],
                                                      None, ALU.subtract))
            s.opc("vector", lambda e: e.tensor_tensor(gmask[:], gd[:], zbc16,
                                                      ALU.is_ge))
            s.opc("vector", lambda e: e.tensor_scalar(negm1[:], m1[:], -1.0,
                                                      None, ALU.mult))
            s.op("vector", lambda e: e.tensor_copy(gd[:], g_sb[:]), sig="g_v2s")
            s.wait("scalar", "g_v2s")
            s.op("scalar", lambda e: e.activation(ge_exp[:], g_sb[:], AF.Exp,
                                                  bias=negm1[:, 0:1]),
                 sig="g_s2v")
            s.wait("vector", "g_s2v")
            s.opc("vector", lambda e: e.tensor_mul(gem[:], ge_exp[:], gmask[:]))
            s.opc("vector", lambda e: e.reduce_sum(ssum[:], gem[:], axis=AXX))
            s.opc("vector", lambda e: e.reciprocal(rs[:], ssum[:]))
            s.opc("vector", lambda e: e.tensor_scalar(route32[:], gem[:],
                                                      rs[:, 0:1], None, ALU.mult))
            s.opc("vector", lambda e: e.tensor_copy(route_b[:], route32[:]))
            s.wait("vector", "hz_vector")
            s.op("vector", lambda e: e.tensor_copy(route_b[:, 0:1],
                                                   route_b[:, 0:1]),
                 sig="route_done")
            s.wait("tensor", "route_done")
            s.wait("tensor", "tr_v2t")
            s.op("tensor", lambda e: e.transpose(ptA[0:16, 0:P], route_b[:],
                                                 ident_bf[:]), sig="tr_t2v")
            s.wait("vector", "tr_t2v")
            s.op("vector",
                 lambda e, t=t: e.tensor_copy(
                     route_fm[0:16, t * P:(t + 1) * P], ptA[0:16, 0:P]),
                 sig="tr_v2t")

    s.wait("sync", "tr_v2t")
    s.dma("sync", route_dram, route_fm[:], sig="route_st")

    for e_i in range(NE0):
        if e_i > 0:
            s.wait("gpsimd", "m_mm", 32 * e_i)
            s.wait("vector", "m_mm", 32 * e_i)
        for h2 in range(4):
            s.wait("sync", "wcast", 4 * e_i + h2)
            if h2 == 0:
                s.wait("sync", "b1cast", e_i)
            s.dma("sync", wstage[:].rearrange("p (k i) -> p k i", k=2),
                  w1p[e_i * D + 256 * h2: e_i * D + 256 * (h2 + 1), :]
                  .rearrange("(k p) i -> p k i", p=P), sig="wld")
            s.wait("gpsimd", "wld")
            s.op("gpsimd",
                 lambda e, h2=h2: e.tensor_copy(
                     w1_bf[:, h2 * 2 * D:(h2 + 1) * 2 * D], wstage[:]),
                 sig="wcast")
        for h2 in range(4):
            s.wait("scalar", "w2cast", 4 * e_i + h2)
            s.dma("scalar", w2stage[:].rearrange("p (k i) -> p k i", k=2),
                  w2p[e_i * D + 256 * h2: e_i * D + 256 * (h2 + 1), :]
                  .rearrange("(k p) i -> p k i", p=P), sig="w2ld")
            s.wait("gpsimd", "w2ld")
            if h2 == 0 and e_i > 0:
                s.wait("gpsimd", "y_mm", 32 * e_i)
            s.op("gpsimd",
                 lambda e, h2=h2: e.tensor_copy(
                     w2_bf[:, h2 * 2 * D:(h2 + 1) * 2 * D], w2stage[:]),
                 sig="w2cast")
        s.wait("sync", "wcast", 4 * (e_i + 1))
        s.dma("sync", wstage[0:1, 0:D], b1p[e_i:e_i + 1, :], sig="b1ld")
        s.wait("vector", "b1ld")
        s.op("vector", lambda e: e.tensor_copy(b1_bf[:], wstage[0:1, 0:D]),
             sig="b1cast")

        s.wait("sync", "route_st", 16)
        if e_i > 0:
            s.wait("sync", "bc_mm", 4 * e_i)
        s.dma("sync", route_row[:], route_dram[e_i:e_i + 1, :], sig="route_ld")
        for g in range(NG):
            if e_i == 0:
                s.wait("tensor", "tbf_done", 32 * (g + 1))
            s.wait("tensor", "bc_done")
            s.wait("tensor", "route_ld", 16 * (e_i + 1))
            s.op("tensor",
                 lambda e, g=g: e.matmul(
                     pbc[:], ones_bf[0:1, :],
                     route_row[0:1, g * GTOK:(g + 1) * GTOK],
                     start=True, stop=True),
                 sig="bc_mm")
            s.wait("vector", "bc_mm")
            s.op("vector", lambda e: e.tensor_scalar(bc_bf[:], pbc[:], SW_SCALE,
                                                     None, ALU.mult),
                 sig="bc_done")
            if g == 0:
                s.wait("tensor", "wcast", 4 * (e_i + 1))
                s.wait("tensor", "b1cast", e_i + 1)
            for ich in range(8):
                bank = pm if ich % 2 == 0 else po
                nev = s.val("m_evac")
                if nev >= 2:
                    s.wait("tensor", "m_evac", nev - 1)
                s.wait("tensor", "y_evac", s.val("y_evac"))
                for k in range(8):
                    s.op("tensor",
                         lambda e, k=k, ich=ich, g=g, b=bank: e.matmul(
                             b[:], w1_bf[:, k * D + ich * P: k * D + ich * P + P],
                             t_bf[:, k * TOK + g * GTOK: k * TOK + (g + 1) * GTOK],
                             start=(k == 0), stop=False))
                s.op("tensor",
                     lambda e, ich=ich, b=bank: e.matmul(
                         b[:], b1_bf[0:1, ich * P:(ich + 1) * P],
                         ones_g_bf[0:1, :],
                         start=False, stop=True),
                     sig="m_mm")
                s.wait("vector", "m_mm")
                s.wait("vector", "s_done")
                s.op("vector", lambda e, b=bank: e.tensor_scalar(
                    m_sb[:], b[:], 7.0, None, ALU.min),
                     sig="m_evac")
                s.wait("scalar", "m_evac")
                s.op("scalar",
                     lambda e, ich=ich: e.activation(
                         s_fm[:, ich * GTOK:(ich + 1) * GTOK], m_sb[:],
                         AF.Silu, scale=1.702),
                     sig="s_done")
            if g == 0:
                s.wait("tensor", "w2cast", 4 * (e_i + 1))
            s.wait("tensor", "s_done")
            s.wait("tensor", "m_evac", s.val("m_evac"))
            for och in range(8):
                bank = po if och % 2 == 0 else pm
                nev = s.val("y_evac")
                if nev >= 2:
                    s.wait("tensor", "y_evac", nev - 1)
                for k in range(8):
                    s.op("tensor",
                         lambda e, k=k, och=och, b=bank: e.matmul(
                             b[:], w2_bf[:, k * D + och * P: k * D + och * P + P],
                             s_fm[:, k * GTOK:(k + 1) * GTOK],
                             start=(k == 0), stop=(k == 7)),
                         sig="y_mm" if k == 7 else None)
                s.wait("vector", "y_mm")
                dst = acc[:, och * TOK + g * GTOK: och * TOK + (g + 1) * GTOK]
                if e_i == 0:
                    s.op("vector",
                         lambda e, d=dst, b=bank: e.tensor_mul(d, b[:], bc_bf[:]),
                         sig="y_evac")
                else:
                    s.op("vector",
                         lambda e, b=bank: e.tensor_mul(y_tmp[:], b[:], bc_bf[:]))
                    s.op("vector",
                         lambda e, d=dst: e.tensor_add(d, d, y_tmp[:]),
                         sig="y_evac")

    s.wait("tensor", "init_cast")
    for g in range(NG):
        for och in range(8):
            s.wait("tensor", "y_evac")
            s.op("tensor",
                 lambda e, g=g, och=och: e.matmul(
                     po[:], b2_bf[0:16, och * P:(och + 1) * P],
                     route_fm[0:16, g * GTOK:(g + 1) * GTOK],
                     start=True, stop=True),
                 sig="y_mm")
            s.wait("vector", "y_mm")
            dst = acc[:, och * TOK + g * GTOK: och * TOK + (g + 1) * GTOK]
            s.op("vector", lambda e, d=dst: e.tensor_add(d, d, po[:]),
                 sig="y_evac")

    s.dma("sync", R[:, 0:D], r1r[:, 0, :], sig="p3ld")
    s.dma("sync", R[:, D:2 * D], r1r[:, 1, :], sig="p3ld")
    for t in range(NT):
        R_t = R[:, (t % 2) * D:(t % 2 + 1) * D]
        s.wait("vector", "p3ld", 16 * (t + 1))
        s.wait("scalar", "p3ld", 16 * (t + 1))
        emit_ln_tile(p, s, R_t, consts, scr, None, out_f32=ln2_t[:],
                     sig="ln_done")
        s.wait("vector", "ln_done")
        s.op("vector", lambda e, Rt=R_t: e.tensor_add(Rt, Rt, ln2_t[:]),
             sig="racc")
        g, ti = divmod(t, 4)
        if t == 0:
            s.wait("tensor", "y_evac")
        s.wait("tensor", "trf_v2t")
        for k in range(8):
            bank = (ptC, ptD)[k % 2]
            if k >= 2:
                s.wait("tensor", "trf_v2t")
            s.op("tensor",
                 lambda e, k=k, bank=bank, g=g, ti=ti: e.transpose(
                     bank[:],
                     acc[:, k * TOK + g * GTOK + ti * P:
                         k * TOK + g * GTOK + ti * P + P],
                     ident_f32[:]),
                 sig="trf_t2v")
            s.wait("vector", "trf_t2v")
            s.op("vector",
                 lambda e, k=k, bank=bank, Rt=R_t: e.tensor_add(
                     Rt[:, k * P:(k + 1) * P], Rt[:, k * P:(k + 1) * P],
                     bank[:]),
                 sig="trf_v2t")
        s.wait("sync", "trf_v2t")
        s.dma("sync", r2r[:, t, :], R_t, sig="rst")
        if t + 2 < NT:
            s.wait("sync", "rst", 16 * (t + 1))
            s.dma("sync", R[:, (t % 2) * D:(t % 2 + 1) * D],
                  r1r[:, t + 2, :], sig="p3ld")

    if DBG_L0F:
        dbg_t = p.out("dbg_t", [P, 8 * TOK], BF16)
        dbg_route = p.out("dbg_route", [16, TOK], BF16)
        dbg_acc = p.out("dbg_acc", [P, 8 * TOK])
        s.dma("sync", dbg_t, t_bf[:], sig="rst")
        s.dma("sync", dbg_route, route_fm[:], sig="rst")
        s.dma("sync", dbg_acc, acc[:], sig="rst")
    s.wait("gpsimd", "rst")
    return p.finish()


def run_l0_ffn(r1_l, scale, gate_w, gate_b, mlp1_w, mlp1_b, mlp2_w, mlp2_b):
    nc = build_l0_ffn()
    scale_fm = np.ascontiguousarray(scale.reshape(8, P).T)
    gate_wT = np.ascontiguousarray(gate_w.T)
    w1p = np.ascontiguousarray(
        mlp1_w[:, ::2, :].transpose(0, 2, 1)).reshape(NE0 * D, D)
    b1p = np.ascontiguousarray(mlp1_b[:, ::2])
    w2p = np.ascontiguousarray(mlp2_w.transpose(0, 2, 1)).reshape(NE0 * D, D)
    base = dict(scale_fm=scale_fm, gate_wT=gate_wT,
                gate_b=gate_b.reshape(1, 16), w1p=w1p, b1p=b1p, w2p=w2p,
                b2=mlp2_b, ident_bf=to_bf16(_hc["ident_bf"]),
                ident_f=_hc["ident_bf"], ones_row=_hc["ones_row"],
                ones_512=_hc["ones_512"], consts=_hc["consts"])
    in_maps = [dict(base, r1=np.ascontiguousarray(r1_l[c]))
               for c in range(NCORES)]
    res = run_bass_kernel_spmd(nc, in_maps, core_ids=list(range(NCORES)))
    return res.results



def build_l1a_qkv():
    p = Prog("l1a_qkv")
    s = p.s
    r2 = p.inp("r2", [TOK, D])
    w_qkv = p.inp("w_qkv", [D, 3 * D])
    b_qkv = p.inp("b_qkv", [1, 3 * D])
    cosr_i = p.inp("cosr", [TOK, 512])
    sinr_i = p.inp("sinr", [TOK, 512])
    ident = p.inp("ident_bf", [P, P], BF16)
    ones_r = p.inp("ones_row", [1, P])
    consts_i = p.inp("consts", [P, 8])

    o_rq = p.out("rq_fm", [D, TOK], BF16)
    o_q = p.out("q_fm", [D, TOK], BF16)
    o_rk = p.out("rk_fm", [D, TOK], BF16)
    o_k = p.out("k_fm", [D, TOK], BF16)
    o_v = p.out("v_tm", [TOK, D], BF16)

    R = p.sb("R", [P, 2 * D])
    wstage = p.sb("wstage", [P, 4 * D])
    wq_bf = p.sb("wq_bf", [P, 8 * 3 * D], BF16)
    bias_bf = p.sb("bias_bf", [1, 3 * D], BF16)
    ident_bf = p.sb("identb", [P, P], BF16)
    ones_bf = p.sb("ones_bf", [1, P], BF16)
    ones_st = p.sb("ones_st", [1, P])
    consts = p.sb("consts_s", [P, 8])
    ln_bf = p.sb("ln_bf", [P, D], BF16)
    lnfm = p.sb("lnfm", [P, 8 * P], BF16)
    q_tm = p.sb("q_tm", [P, D])
    k_tm = p.sb("k_tm", [P, D])
    rq_tm = p.sb("rq_tm", [P, D])
    rk_tm = p.sb("rk_tm", [P, D])
    v_bf = p.sb("v_bf", [P, D], BF16)
    cast_bf = p.sb("cast_bf", [P, D], BF16)
    fmbuf = p.sb("fmbuf", [P, 8 * P], BF16)
    cos_t = p.sb("cos_t", [P, 512])
    sin_t = p.sb("sin_t", [P, 512])
    mt1 = p.sb("mt1", [P, 512])
    mt2 = p.sb("mt2", [P, 512])
    tmp1 = p.sb("tmp1", [P, 512])
    tmp2 = p.sb("tmp2", [P, 512])
    scr = {k: p.sb("scr_" + k, [P, 1])
           for k in ["ss", "sum", "mean", "ms", "var", "std", "rstd"]}
    scr["sq"] = p.sb("scr_sq", [P, D])

    ptA = p.ps("ptA", [P, P], BF16)
    ptB = p.ps("ptB", [P, P], BF16)
    pa0 = p.ps("pa0", [P, 512])
    pa1 = p.ps("pa1", [P, 512])

    r2r = r2.rearrange("(t p) d -> p t d", p=P)
    vr = o_v.rearrange("(t p) d -> p t d", p=P)
    cosr = cosr_i.rearrange("(t p) j -> p t j", p=P)
    sinr = sinr_i.rearrange("(t p) j -> p t j", p=P)
    fm_outs = {"rq": o_rq, "q": o_q, "rk": o_rk, "k": o_k}
    fm_r = {k: v.rearrange("(c p) T -> p c T", p=P) for k, v in fm_outs.items()}

    s.dma("sync", ident_bf[:], ident, sig="dld")
    s.dma("sync", ones_st[:], ones_r, sig="dld")
    s.dma("sync", consts[:], consts_i, sig="dld")
    s.dma("sync", wstage[0:1, 0:3 * D], b_qkv, sig="dld")
    s.dma("sync", R[:, 0:D], r2r[:, 0, :], sig="p1ld")
    s.dma("sync", R[:, D:2 * D], r2r[:, 1, :], sig="p1ld")
    s.wait("vector", "dld")
    s.op("vector", lambda e: e.tensor_copy(ones_bf[:], ones_st[:]))
    s.op("vector", lambda e: e.tensor_copy(bias_bf[:], wstage[0:1, 0:3 * D]),
         sig="cast_done")
    npieces = 0
    for rh in range(2):
        for ch in range(3):
            s.wait("sync", "cast_done", 1 + npieces)
            s.dma("sync", wstage[:].rearrange("p (k o) -> p k o", k=4),
                  w_qkv[512 * rh:512 * (rh + 1), ch * D:(ch + 1) * D]
                  .rearrange("(k p) o -> p k o", p=P), sig="wld")
            s.wait("vector", "wld")
            def cast_piece(e, rh=rh, ch=ch):
                for kk in range(4):
                    e.tensor_copy(
                        wq_bf[:, (4 * rh + kk) * 3 * D + ch * D:
                              (4 * rh + kk) * 3 * D + (ch + 1) * D],
                        wstage[:, kk * D:(kk + 1) * D])
                return e.tensor_copy(bias_bf[:, 0:1], bias_bf[:, 0:1])
            s.op("vector", cast_piece, sig="cast_done")
            npieces += 1
    s.wait("scalar", "p1ld", 16)

    for t in range(NT):
        R_t = R[:, (t % 2) * D:(t % 2 + 1) * D]
        s.wait("vector", "p1ld", 16 * (t + 1))
        s.wait("scalar", "p1ld", 16 * (t + 1))
        emit_ln_tile(p, s, R_t, consts, scr, ln_bf[:], sig="ln_done")
        if t + 2 < NT:
            s.wait("sync", "ln_done", t + 1)
            s.dma("sync", R[:, (t % 2) * D:(t % 2 + 1) * D],
                  r2r[:, t + 2, :], sig="p1ld")
        s.wait("sync", "rope_done", 2 * t)
        s.wait("vector", "vst", 16 * t)
        s.dma("sync", cos_t[:], cosr[:, t, :], sig="cld")
        s.dma("sync", sin_t[:], sinr[:, t, :], sig="cld")
        s.wait("tensor", "ln_done")
        emit_transpose8(p, s, ln_bf, lnfm, ident_bf, (ptA, ptB))
        s.wait("tensor", "tr_v2t")
        if t == 0:
            s.wait("tensor", "cast_done", 7)
        for half in range(6):
            pshalf = pa0 if half % 2 == 0 else pa1
            s.wait("tensor", "phi_v2s")
            s.wait("tensor", "evac_done")
            for k in range(8):
                s.op("tensor",
                     lambda e, k=k, half=half, ps_=pshalf: e.matmul(
                         ps_[:], lnfm[:, k * P:(k + 1) * P],
                         wq_bf[:, k * 3 * D + half * 512:
                               k * 3 * D + half * 512 + 512],
                         start=(k == 0), stop=False))
            s.op("tensor",
                 lambda e, half=half, ps_=pshalf: e.matmul(
                     ps_[:], ones_bf[0:1, :],
                     bias_bf[0:1, half * 512:half * 512 + 512],
                     start=False, stop=True),
                 sig="proj_mm")
            if half < 4:
                dst = (q_tm if half < 2 else k_tm)
                s.wait("scalar", "proj_mm")
                emit_phi_evac(p, s, pshalf, tmp1, tmp2, None,
                              out_f32=dst[:, (half % 2) * 512:
                                          (half % 2) * 512 + 512])
            else:
                s.wait("vector", "proj_mm")
                s.op("vector",
                     lambda e, half=half, ps_=pshalf: e.tensor_copy(
                         v_bf[:, (half - 4) * 512:(half - 4) * 512 + 512],
                         ps_[:]),
                     sig="evac_done")
        s.wait("sync", "evac_done")
        s.dma("sync", vr[:, t, :], v_bf[:], sig="vst")
        s.wait("vector", "cld", 32 * (t + 1))
        for src_tm, rot_tm_ in ((q_tm, rq_tm), (k_tm, rk_tm)):
            x1 = src_tm[:, 0:D:2]
            x2 = src_tm[:, 1:D:2]
            s.op("vector", lambda e, x1=x1: e.tensor_mul(mt1[:], x1, cos_t[:]))
            s.op("vector", lambda e, x2=x2: e.tensor_mul(mt2[:], x2, sin_t[:]))
            s.op("vector", lambda e, r=rot_tm_: e.tensor_sub(
                r[:, 0:D:2], mt1[:], mt2[:]))
            s.op("vector", lambda e, x1=x1: e.tensor_mul(mt1[:], x1, sin_t[:]))
            s.op("vector", lambda e, x2=x2: e.tensor_mul(mt2[:], x2, cos_t[:]))
            s.op("vector", lambda e, r=rot_tm_: e.tensor_add(
                r[:, 1:D:2], mt1[:], mt2[:]), sig="rope_done")
        for name, src_tm in (("rq", rq_tm), ("q", q_tm), ("rk", rk_tm),
                             ("k", k_tm)):
            s.wait("vector", "tr_t2v")
            s.wait("vector", "fst")
            s.op("vector", lambda e, st_=src_tm: e.tensor_copy(cast_bf[:], st_[:]),
                 sig="castq")
            s.wait("tensor", "castq")
            emit_transpose8(p, s, cast_bf, fmbuf, ident_bf, (ptA, ptB))
            s.wait("sync", "tr_v2t")
            s.dma("sync", fm_r[name][:, :, t * P:(t + 1) * P],
                  fmbuf[:].rearrange("p (c q) -> p c q", c=8), sig="fst")
    s.wait("gpsimd", "vst")
    return p.finish()



NH = 16
HD = 64
SEQ = 4 * TOK
NKT = SEQ // P


def build_l1a_attn():
    p = Prog("l1a_attn")
    s = p.s
    r2 = p.inp("r2", [TOK, D])
    rqf = p.inp("rqf", [D, TOK], BF16)
    qf = p.inp("qf", [D, TOK], BF16)
    rk4 = p.inp("rk4", [4 * D, TOK], BF16)
    k4 = p.inp("k4", [4 * D, TOK], BF16)
    v4 = p.inp("v4", [SEQ, D], BF16)
    w_out = p.inp("w_out", [D, D])
    b_out = p.inp("b_out", [1, D])
    ones_r = p.inp("ones_row", [1, P])

    o_r3 = p.out("r3", [TOK, D])

    R = p.sb("R", [P, 2 * D])
    attn_fm = p.sb("attn_fm", [P, 8 * TOK], BF16)
    kq = [p.sb(f"kq{i}", [P, SEQ], BF16) for i in range(2)]
    v_h = [p.sb(f"v_h{i}", [P, NKT * HD], BF16) for i in range(2)]
    rq = [p.sb(f"rq{i}", [P, TOK], BF16) for i in range(2)]
    wstage = p.sb("wstage", [P, 4 * D])
    wo_bf = p.sb("wo_bf", [P, 8 * D], BF16)
    bias_bf = p.sb("bias_bf", [1, D], BF16)
    ones_bf = p.sb("ones_bf", [1, P], BF16)
    ones_st = p.sb("ones_st", [1, P])
    rc = [p.sb(f"rc{i}", [P, 512]) for i in range(3)]
    wT = [p.sb(f"wT{i}", [P, 512], BF16) for i in range(3)]

    psN = [p.ps(f"psN{i}", [P, 512]) for i in range(3)]
    psD = [p.ps(f"psD{i}", [P, 512]) for i in range(3)]
    psO = p.ps("psO", [P, 512])

    r2r = r2.rearrange("(t p) d -> p t d", p=P)
    r3r = o_r3.rearrange("(t p) d -> p t d", p=P)

    s.dma("sync", ones_st[:], ones_r, sig="dld")
    s.dma("sync", wstage[0:1, 0:D], b_out, sig="dld")
    s.wait("vector", "dld")
    s.op("vector", lambda e: e.tensor_copy(ones_bf[:], ones_st[:]))
    s.op("vector", lambda e: e.tensor_copy(bias_bf[:], wstage[0:1, 0:D]),
         sig="cast_done")
    for h2 in range(2):
        s.wait("sync", "cast_done", 1 + h2)
        s.dma("sync", wstage[:].rearrange("p (k o) -> p k o", k=4),
              w_out[512 * h2:512 * (h2 + 1), :].rearrange("(k p) o -> p k o", p=P),
              sig="wld")
        s.wait("vector", "wld")
        s.op("vector",
             lambda e, h2=h2: e.tensor_copy(
                 wo_bf[:, h2 * 4 * D:(h2 + 1) * 4 * D], wstage[:]),
             sig="cast_done")

    def head_loads(h):
        hb = h % 2
        for j in range(4):
            s.dma("sync", kq[hb][0:64, j * TOK:(j + 1) * TOK],
                  rk4[j * D + h * HD: j * D + (h + 1) * HD, :], sig="hld")
            s.dma("sync", kq[hb][64:128, j * TOK:(j + 1) * TOK],
                  k4[j * D + h * HD: j * D + (h + 1) * HD, :], sig="hld")
            s.dma("sync",
                  v_h[hb][:, j * 16 * HD:(j + 1) * 16 * HD]
                  .rearrange("p (kt c) -> p kt c", kt=16),
                  v4[j * TOK:(j + 1) * TOK, h * HD:(h + 1) * HD]
                  .rearrange("(kt p) c -> p kt c", p=P), sig="hld")
        s.dma("sync", rq[hb][0:64, :], rqf[h * HD:(h + 1) * HD, :], sig="hld")
        s.dma("sync", rq[hb][64:128, :], qf[h * HD:(h + 1) * HD, :], sig="hld")

    head_loads(0)
    head_loads(1)

    IT = NH * 4 * NKT
    PER_H = 4 * NKT

    def stage(it):
        h, r = divmod(it, PER_H)
        g, kt = divmod(r, NKT)
        return h, g, kt

    def emit_wv(jt):
        h, g, kt = stage(jt)
        ob = 64 * (h % 2)
        G = h * 4 + g
        if kt == 0:
            prev = G - 1 if g > 0 else G - 5
            if prev >= 0:
                s.wait("tensor", "oev", prev + 1)
        s.op("tensor",
             lambda e, h=h, kt=kt, ob=ob, jt=jt: e.matmul(
                 psO[ob:ob + 64, :], v_h[h % 2][:, kt * HD:(kt + 1) * HD],
                 wT[jt % 3][:], start=(kt == 0), stop=(kt == NKT - 1),
                 tile_position=(0, ob), skip_group_check=True),
             sig="wv_mm")
        if kt == NKT - 1:
            s.wait("vector", "wv_mm", jt + 1)
            s.op("vector",
                 lambda e, h=h, g=g, ob=ob: e.tensor_copy(
                     attn_fm[ob:ob + 64,
                             (h // 2) * TOK + g * 512:(h // 2) * TOK + g * 512 + 512],
                     psO[ob:ob + 64, :]),
                 sig="oev")

    for it in range(IT):
        h, g, kt = stage(it)
        hb = h % 2
        pi = it % 3
        if g == 0 and kt == 0:
            s.wait("tensor", "hld", 16 * 14 * (h + 1))
            if h + 1 < NH:
                if h >= 1:
                    s.wait("sync", "nd_mm", PER_H * h)
                    s.wait("sync", "wv_mm", PER_H * h)
                head_loads(h + 1)
        if it >= 3:
            s.wait("tensor", "w_done", it - 2)
            s.wait("tensor", "rc_done", it - 2)
        s.op("tensor",
             lambda e, pi=pi, kt=kt, g=g, hb=hb: e.matmul(
                 psN[pi][:], kq[hb][0:64, kt * P:(kt + 1) * P],
                 rq[hb][0:64, g * 512:(g + 1) * 512],
                 start=True, stop=True, tile_position=(0, 0)))
        s.op("tensor",
             lambda e, pi=pi, kt=kt, g=g, hb=hb: e.matmul(
                 psD[pi][:], kq[hb][64:128, kt * P:(kt + 1) * P],
                 rq[hb][64:128, g * 512:(g + 1) * 512],
                 start=True, stop=True, tile_position=(64, 0)),
             sig="nd_mm")
        if it >= 2:
            s.wait("tensor", "w_done", it - 1)
            emit_wv(it - 2)
        if it >= 3:
            s.wait("scalar", "w_done", it - 2)
        s.wait("scalar", "nd_mm", it + 1)
        s.op("scalar",
             lambda e, pi=pi: p.act_recip(e, rc[pi][:], psD[pi][:]),
             sig="rc_done")
        if it >= 3:
            s.wait("vector", "wv_mm", it - 2)
        s.wait("vector", "rc_done", it + 1)
        s.op("vector",
             lambda e, pi=pi: e.tensor_mul(wT[pi][:], psN[pi][:], rc[pi][:]),
             sig="w_done")
    for jt in (IT - 2, IT - 1):
        s.wait("tensor", "w_done", jt + 1)
        emit_wv(jt)

    s.dma("sync", R[:, 0:D], r2r[:, 0, :], sig="p3ld")
    s.dma("sync", R[:, D:2 * D], r2r[:, 1, :], sig="p3ld")
    s.wait("tensor", "oev")
    for t in range(NT):
        R_t = R[:, (t % 2) * D:(t % 2 + 1) * D]
        for half in range(2):
            pshalf = psN[half]
            if not (t == 0 and half == 0):
                s.wait("tensor", "r_done")
            for k in range(8):
                s.op("tensor",
                     lambda e, k=k, half=half, t=t, ps_=pshalf: e.matmul(
                         ps_[:], attn_fm[:, k * TOK + t * P:k * TOK + (t + 1) * P],
                         wo_bf[:, k * D + half * 512:k * D + half * 512 + 512],
                         start=(k == 0), stop=False))
            s.op("tensor",
                 lambda e, half=half, ps_=pshalf: e.matmul(
                     ps_[:], ones_bf[0:1, :],
                     bias_bf[0:1, half * 512:half * 512 + 512],
                     start=False, stop=True),
                 sig="o_mm")
            s.wait("vector", "o_mm")
            s.wait("vector", "p3ld", 16 * (t + 1))
            s.op("vector",
                 lambda e, half=half, Rt=R_t, ps_=pshalf: e.tensor_add(
                     Rt[:, half * 512:half * 512 + 512],
                     Rt[:, half * 512:half * 512 + 512], ps_[:]),
                 sig="r_done")
        s.wait("sync", "r_done")
        s.dma("sync", r3r[:, t, :], R_t, sig="rst")
        if t + 2 < NT:
            s.wait("sync", "rst", 16 * (t + 1))
            s.dma("sync", R[:, (t % 2) * D:(t % 2 + 1) * D],
                  r2r[:, t + 2, :], sig="p3ld")
    s.wait("gpsimd", "rst")
    return p.finish()


def run_l1a_attn(r2_l, qk_res, w_out, b_out):
    nc = build_l1a_attn()
    base = dict(w_out=np.ascontiguousarray(w_out), b_out=b_out.reshape(1, D),
                ones_row=_hc["ones_row"])
    in_maps = []
    for c in range(NCORES):
        grp = (c // 4) * 4
        peers = [grp + j for j in range(4)]
        rk4 = np.concatenate([np.asarray(qk_res[j]["rk_fm"]) for j in peers], 0)
        k4 = np.concatenate([np.asarray(qk_res[j]["k_fm"]) for j in peers], 0)
        v4 = np.concatenate([np.asarray(qk_res[j]["v_tm"]) for j in peers], 0)
        in_maps.append(dict(base, r2=np.ascontiguousarray(r2_l[c]),
                            rqf=np.asarray(qk_res[c]["rq_fm"]),
                            qf=np.asarray(qk_res[c]["q_fm"]),
                            rk4=rk4, k4=k4, v4=v4))
    res = run_bass_kernel_spmd(nc, in_maps, core_ids=list(range(NCORES)))
    return res.results



NE1 = 8
DFF = 4096


def build_l1_ffn():
    p = Prog("l1_ffn")
    s = p.s
    r3 = p.inp("r3", [TOK, D])
    w1_i = p.inp("w1", [NE1 * D, DFF])
    b1_i = p.inp("b1", [NE1, DFF])
    w2_i = p.inp("w2", [NE1 * DFF, D])
    b2_i = p.inp("b2", [NE1, D])
    ident = p.inp("ident_bf", [P, P], BF16)
    ones_r = p.inp("ones_row", [1, P])
    ones5_i = p.inp("ones_512", [1, GTOK])
    ladd24_i = p.inp("ladd24", [P, 24])
    mod6_i = p.inp("mod6", [P, 6])
    eq9_i = p.inp("eq9", [P, 9])
    consts_i = p.inp("consts", [P, 8])

    o_r4 = p.out("r4", [TOK, D])
    route_dram = p.nc.dram_tensor("route_dram1", [NE1, TOK], BF16).ap()

    acc = p.sb("acc", [P, 8 * TOK], BF16)
    t_bf = p.sb("t_bf", [P, 8 * TOK], BF16)
    s_half = p.sb("s_half", [P, 16 * TOK], BF16)
    R = s_half[:, 12 * TOK:12 * TOK + 4 * D].bitcast(F32)
    w1_bf = p.sb("w1_bf", [P, 8 * 2048], BF16)
    w2_bf = [p.sb(f"w2_bf{i}", [P, 16 * P], BF16) for i in range(2)]
    w2stage = p.sb("w2stage", [P, 8 * P])
    wstage = p.sb("wstage", [P, 2 * D])
    ident_bf = p.sb("identb", [P, P], BF16)
    ones_bf = p.sb("ones_bf", [1, P], BF16)
    ones_g_bf = p.sb("ones_g_bf", [1, 2 * GTOK], BF16)
    ones_st = p.sb("ones_st", [1, P])
    consts = p.sb("consts_s", [P, 8])
    ladd24 = p.sb("ladd24_s", [P, 24])
    mod6 = p.sb("mod6_s", [P, 6])
    eq9 = p.sb("eq9_s", [P, 9])
    b1_bf = p.sb("b1_bf", [1, DFF // 2], BF16)
    b2_bf = p.sb("b2_bf", [NE1, D], BF16)
    route_fm = p.sb("route_fm", [NE1, TOK], BF16)
    route_row = p.sb("route_row", [1, TOK], BF16)
    bc_bf2 = [p.sb(f"bc_bf{i}", [P, 2 * GTOK], BF16) for i in range(2)]
    t0 = p.sb("t0", [P, 1]); av = p.sb("av", [P, 1]); f24 = p.sb("f24", [P, 24])
    fv = p.sb("fv", [P, 1]); sgn = p.sb("sgn", [P, 1]); trn = p.sb("trn", [P, 1])
    uu = p.sb("uu", [P, 1]); m6 = p.sb("m6", [P, 6]); q6 = p.sb("q6", [P, 1])
    hh = p.sb("hh", [P, 1]); ge9 = p.sb("ge9", [P, 9]); i8 = p.sb("i8", [P, 8])
    route32 = p.sb("route32", [P, 8]); route_b = p.sb("route_b", [P, 8], BF16)
    scr = {k: p.sb("scr_" + k, [P, 1])
           for k in ["ss", "sum", "mean", "ms", "var", "std", "rstd"]}
    scr["sq"] = s_half[:, D:3 * D].bitcast(F32)
    y_tmp = p.sb("y_tmp", [P, GTOK])
    ln_bf = s_half[:, 0:D]
    ln4_t = wstage[:, 0:D]

    ptA = p.ps("ptA", [P, P], BF16)
    ptB = p.ps("ptB", [P, P], BF16)
    pm = p.ps("pm", [P, GTOK])
    po = p.ps("po", [P, GTOK])
    pbc = p.ps("pbc", [P, GTOK])

    r3r = r3.rearrange("(t p) d -> p t d", p=P)
    r4r = o_r4.rearrange("(t p) d -> p t d", p=P)

    s.dma("sync", ident_bf[:], ident, sig="dld")
    s.dma("sync", ones_st[:], ones_r, sig="dld")
    s.dma("sync", wstage[0:1, D:D + GTOK], ones5_i, sig="dld")
    s.dma("sync", wstage[0:1, D + GTOK:D + 2 * GTOK], ones5_i, sig="dld")
    s.dma("sync", consts[:], consts_i, sig="dld")
    s.dma("sync", ladd24[:], ladd24_i, sig="dld")
    s.dma("sync", mod6[:], mod6_i, sig="dld")
    s.dma("sync", eq9[:], eq9_i, sig="dld")
    s.dma("sync", wstage[0:NE1, 0:D], b2_i, sig="dld")
    s.dma("sync", R[:, 0:D], r3r[:, 0, :], sig="p1ld")
    s.dma("sync", R[:, D:2 * D], r3r[:, 1, :], sig="p1ld")
    s.wait("vector", "dld")
    s.op("vector", lambda e: e.tensor_copy(ones_bf[:], ones_st[:]))
    s.op("vector", lambda e: e.tensor_copy(ones_g_bf[:],
                                           wstage[0:1, D:D + 2 * GTOK]))
    s.op("vector", lambda e: e.tensor_copy(b2_bf[:], wstage[0:NE1, 0:D]),
         sig="init_cast")
    s.wait("scalar", "p1ld", 16)

    for t in range(NT):
        R_t = R[:, (t % 2) * D:(t % 2 + 1) * D]
        s.wait("vector", "p1ld", 16 * (t + 1))
        s.wait("scalar", "p1ld", 16 * (t + 1))
        emit_ln_tile(p, s, R_t, consts, scr, None, out_f32=ln4_t[:],
                     sig="ln_done")
        if t + 2 < NT:
            s.wait("sync", "ln_done", t + 1)
            s.dma("sync", R[:, (t % 2) * D:(t % 2 + 1) * D],
                  r3r[:, t + 2, :], sig="p1ld")
        s.opc("vector", lambda e: e.tensor_add(t0[:], ln4_t[:, 0:1],
                                               ln4_t[:, 1:2]))
        s.wait("vector", "hz_vector")
        s.op("vector", lambda e: e.tensor_copy(sgn[:], t0[:]), sig="rt_v2s")
        s.wait("scalar", "rt_v2s")
        s.op("scalar", lambda e: e.activation(av[:], t0[:], AF.Abs))
        s.op("scalar", lambda e: e.sign(sgn[:], t0[:],
                                        bias=consts[:, C_ZERO:C_ZERO + 1]),
             sig="rt_s2v")
        s.wait("vector", "rt_s2v")
        s.opc("vector", lambda e: e.tensor_tensor(
            f24[:], av[:, 0:1].broadcast_to([P, 24]), ladd24[:], ALU.is_ge))
        s.opc("vector", lambda e: e.reduce_sum(fv[:], f24[:], axis=AXX))
        s.opc("vector", lambda e: e.tensor_mul(trn[:], fv[:], sgn[:]))
        s.opc("vector", lambda e: e.tensor_scalar(uu[:], trn[:], 24.0, None,
                                                  ALU.add))
        s.opc("vector", lambda e: e.tensor_tensor(
            m6[:], uu[:, 0:1].broadcast_to([P, 6]), mod6[:], ALU.is_ge))
        s.opc("vector", lambda e: e.reduce_sum(q6[:], m6[:], axis=AXX))
        s.opc("vector", lambda e: e.scalar_tensor_tensor(
            hh[:], q6[:], -8.0, uu[:], ALU.mult, ALU.add))
        s.opc("vector", lambda e: e.tensor_tensor(
            ge9[:], hh[:, 0:1].broadcast_to([P, 9]), eq9[:], ALU.is_ge))
        s.opc("vector", lambda e: e.tensor_sub(i8[:], ge9[:, 0:8], ge9[:, 1:9]))
        s.opc("vector", lambda e: e.tensor_add(route32[:, 1:8], i8[:, 1:8],
                                               i8[:, 0:7]))
        s.opc("vector", lambda e: e.tensor_add(route32[:, 0:1], i8[:, 0:1],
                                               i8[:, 7:8]))
        s.opc("vector", lambda e: e.tensor_copy(route_b[:], route32[:]))
        s.wait("vector", "hz_vector")
        s.wait("vector", "tr_t2v")
        s.op("vector", lambda e: e.tensor_copy(ln_bf[:], ln4_t[:]),
             sig="route_done")
        s.wait("tensor", "route_done")
        s.wait("tensor", "tr_v2t")
        s.op("tensor", lambda e: e.transpose(ptA[0:8, 0:P], route_b[:],
                                             ident_bf[:]), sig="tr_t2v")
        s.wait("vector", "tr_t2v")
        s.op("vector",
             lambda e, t=t: e.tensor_copy(route_fm[0:8, t * P:(t + 1) * P],
                                          ptA[0:8, 0:P]),
             sig="tr_v2t")
        emit_transpose8(p, s, ln_bf, None, ident_bf, (ptA, ptB),
                        evac=lambda e, k, bank, t=t: e.tensor_copy(
                            t_bf[:, k * TOK + t * P: k * TOK + (t + 1) * P],
                            bank[:]))
    s.wait("sync", "tr_v2t")
    s.dma("sync", route_dram, route_fm[:], sig="route_st")

    def emit_w1b1(e_j, ih_j):
        s.wait("sync", "b1cast", s.val("b1cast"))
        s.wait("sync", "w1cast", s.val("w1cast"))
        s.dma("sync", wstage[0:1, 0:2048],
              b1_i[e_j:e_j + 1, ih_j * 2048:(ih_j + 1) * 2048], sig="b1ld")
        s.wait("vector", "b1ld")
        s.wait("vector", "m_mm", s.val("m_mm"))
        s.op("vector",
             lambda e: e.tensor_copy(b1_bf[0:1, :], wstage[0:1, 0:2048]),
             sig="b1cast")
        for pc in range(16):
            dc, ch = divmod(pc, 2)
            half = wstage[:, (pc % 2) * D:(pc % 2 + 1) * D]
            if pc < 2:
                s.wait("sync", "b1cast", s.val("b1cast"))
                s.wait("sync", "w1cast", s.val("w1cast"))
            else:
                s.wait("sync", "w1cast", s.val("w1cast") - 1)
            s.dma("sync", half,
                  w1_i[e_j * D + 128 * dc: e_j * D + 128 * (dc + 1),
                       ih_j * 2048 + ch * D: ih_j * 2048 + (ch + 1) * D],
                  sig="w1ld")
            s.wait("gpsimd", "w1ld")
            if pc == 0:
                s.wait("gpsimd", "m_mm", s.val("m_mm"))
            s.op("gpsimd",
                 lambda e, dc=dc, ch=ch, half=half: e.tensor_copy(
                     w1_bf[:, dc * 2048 + ch * D: dc * 2048 + (ch + 1) * D],
                     half),
                 sig="w1cast")

    nw1 = 0
    for e_i in range(NE1):
        s.wait("sync", "route_st", 16)
        if e_i > 0:
            s.wait("sync", "bc_mm", 4 * e_i)
        s.dma("sync", route_row[:], route_dram[e_i:e_i + 1, :], sig="route_ld")
        s.wait("tensor", "route_ld", 16 * (e_i + 1))
        for g in range(NG):
            s.wait("tensor", "bc_done")
            s.op("tensor",
                 lambda e, g=g: e.matmul(
                     pbc[:], ones_bf[0:1, :],
                     route_row[0:1, g * GTOK:(g + 1) * GTOK],
                     start=True, stop=True),
                 sig="bc_mm")
            s.wait("vector", "bc_mm")
            s.wait("vector", "y_evac", s.val("y_evac"))
            s.op("vector", lambda e, g=g: e.tensor_scalar(
                bc_bf2[g // 2][:, (g % 2) * GTOK:(g % 2 + 1) * GTOK],
                pbc[:], 0.5, None, ALU.mult), sig="bc_done")
        s.wait("sync", "init_cast", 1)
        if e_i == 0:
            s.wait("sync", "route_st", 16)
            s.wait("sync", "tr_v2t")
        for ihalf in range(2):
            if e_i == 0 and ihalf == 0:
                emit_w1b1(0, 0)
            s.wait("scalar", "y_mm", 64 * e_i + 32 * ihalf)
            for g in range(NG):
                if e_i == 0 and ihalf == 0:
                    s.wait("tensor", "tr_v2t")
                s.wait("tensor", "w1cast", s.val("w1cast"))
                s.wait("tensor", "b1cast", 2 * e_i + ihalf + 1)
                for ich in range(16):
                    bank = pm if ich % 2 == 0 else po
                    nev = s.val("m_evac")
                    if nev >= 2:
                        s.wait("tensor", "m_evac", nev - 1)
                    s.wait("tensor", "y_evac", s.val("y_evac"))
                    for k in range(8):
                        s.op("tensor",
                             lambda e, k=k, ich=ich, g=g, b=bank:
                             e.matmul(
                                 b[:],
                                 w1_bf[:, k * 2048 + ich * P: k * 2048 + ich * P + P],
                                 t_bf[:, k * TOK + g * GTOK: k * TOK + (g + 1) * GTOK],
                                 start=(k == 0), stop=False))
                    s.op("tensor",
                         lambda e, ich=ich, b=bank: e.matmul(
                             b[:], b1_bf[0:1, ich * P:(ich + 1) * P],
                             ones_g_bf[0:1, 0:GTOK], start=False, stop=True),
                         sig="m_mm")
                    s.wait("scalar", "m_mm")
                    s.op("scalar",
                         lambda e, ich=ich, g=g, b=bank: e.activation(
                             s_half[:, ich * TOK + g * GTOK:
                                    ich * TOK + (g + 1) * GTOK],
                             b[:], AF.Relu),
                         sig="m_evac")
            def w2_load(och, e_i=e_i, ihalf=ihalf):
                base = e_i * DFF + ihalf * 2048
                for hh in range(2):
                    s.wait("scalar", "w2cast", s.val("w2cast"))
                    s.dma("scalar", w2stage[:].rearrange("p (k o) -> p k o", k=8),
                          w2_i[base + hh * 1024: base + (hh + 1) * 1024,
                               och * P:(och + 1) * P]
                          .rearrange("(k p) o -> p k o", p=P), sig="w2ld")
                    s.wait("gpsimd", "w2ld")
                    if hh == 0:
                        s.wait("gpsimd", "y_mm", s.val("y_mm"))
                    s.op("gpsimd",
                         lambda e, wbuf=w2_bf[och % 2], hh=hh: e.tensor_copy(
                             wbuf[:, hh * 8 * P:(hh + 1) * 8 * P], w2stage[:]),
                         sig="w2cast")

            w2_load(0)
            w2_load(1)
            for och in range(8):
                wbuf = w2_bf[och % 2]
                s.wait("tensor", "w2cast", s.val("w2cast") - (2 if och < 7 else 0))
                for g in range(NG):
                    bank = po if (och * NG + g) % 2 == 0 else pm
                    nev = s.val("y_evac")
                    if nev >= 2:
                        s.wait("tensor", "y_evac", nev - 1)
                    s.wait("tensor", "m_evac", s.val("m_evac"))
                    for k in range(16):
                        s.op("tensor",
                             lambda e, k=k, och=och, g=g, wbuf=wbuf, b=bank:
                             e.matmul(
                                 b[:], wbuf[:, k * P:(k + 1) * P],
                                 s_half[:, k * TOK + g * GTOK:
                                        k * TOK + (g + 1) * GTOK],
                                 start=(k == 0), stop=(k == 15)),
                             sig="y_mm" if k == 15 else None)
                    s.wait("vector", "y_mm")
                    dst = acc[:, och * TOK + g * GTOK: och * TOK + (g + 1) * GTOK]
                    if e_i == 0 and ihalf == 0:
                        s.op("vector",
                             lambda e, d=dst, g=g, b=bank: e.tensor_mul(
                                 d, b[:],
                                 bc_bf2[g // 2][:, (g % 2) * GTOK:
                                                (g % 2 + 1) * GTOK]),
                             sig="y_evac")
                    else:
                        s.op("vector",
                             lambda e, g=g, b=bank: e.tensor_mul(
                                 y_tmp[:], b[:],
                                 bc_bf2[g // 2][:, (g % 2) * GTOK:
                                                (g % 2 + 1) * GTOK]))
                        s.op("vector",
                             lambda e, d=dst: e.tensor_add(d, d, y_tmp[:]),
                             sig="y_evac")
                if och + 2 < 8:
                    w2_load(och + 2)
                if och == 0:
                    if ihalf == 0:
                        emit_w1b1(e_i, 1)
                    elif e_i + 1 < NE1:
                        emit_w1b1(e_i + 1, 0)

    s.wait("tensor", "init_cast")
    for g in range(NG):
        for och in range(8):
            bank = po if (g * 8 + och) % 2 == 0 else pm
            nev = s.val("y_evac")
            if nev >= 2:
                s.wait("tensor", "y_evac", nev - 1)
            s.op("tensor",
                 lambda e, g=g, och=och, b=bank: e.matmul(
                     b[:], b2_bf[0:NE1, och * P:(och + 1) * P],
                     route_fm[0:NE1, g * GTOK:(g + 1) * GTOK],
                     start=True, stop=True),
                 sig="y_mm")
            s.wait("vector", "y_mm")
            dst = acc[:, och * TOK + g * GTOK: och * TOK + (g + 1) * GTOK]
            s.op("vector", lambda e, d=dst, b=bank: e.tensor_add(d, d, b[:]),
                 sig="y_evac")

    s.wait("sync", "y_mm", s.val("y_mm"))
    s.dma("sync", R[:, 0:D], r3r[:, 0, :], sig="p3ld")
    s.dma("sync", R[:, D:2 * D], r3r[:, 1, :], sig="p3ld")
    for t in range(NT):
        R_t = R[:, (t % 2) * D:(t % 2 + 1) * D]
        s.wait("vector", "p3ld", 16 * (t + 1))
        g, ti = divmod(t, 4)
        if t == 0:
            s.wait("tensor", "y_evac")
        s.wait("tensor", "tr_v2t")
        for k in range(8):
            bank = (ptA, ptB)[k % 2]
            if k >= 2:
                s.wait("tensor", "tr_v2t")
            s.op("tensor",
                 lambda e, k=k, bank=bank, t=t: e.transpose(
                     bank[:], acc[:, k * TOK + t * P: k * TOK + (t + 1) * P],
                     ident_bf[:]),
                 sig="tr_t2v")
            s.wait("vector", "tr_t2v")
            s.op("vector",
                 lambda e, k=k, bank=bank, Rt=R_t: e.tensor_add(
                     Rt[:, k * P:(k + 1) * P], Rt[:, k * P:(k + 1) * P],
                     bank[:]),
                 sig="tr_v2t")
        s.wait("sync", "tr_v2t")
        s.dma("sync", r4r[:, t, :], R_t, sig="rst")
        if t + 2 < NT:
            s.wait("sync", "rst", 16 * (t + 1))
            s.dma("sync", R[:, (t % 2) * D:(t % 2 + 1) * D],
                  r3r[:, t + 2, :], sig="p3ld")

    s.wait("gpsimd", "rst")
    return p.finish()


def run_l1_ffn(r3_l, W1, b1, W2, b2):
    nc = build_l1_ffn()
    base = dict(w1=np.ascontiguousarray(W1).reshape(NE1 * D, DFF),
                b1=np.ascontiguousarray(b1),
                w2=np.ascontiguousarray(W2).reshape(NE1 * DFF, D),
                b2=np.ascontiguousarray(b2),
                ident_bf=to_bf16(_hc["ident_bf"]), ones_row=_hc["ones_row"],
                ones_512=_hc["ones_512"], ladd24=_hc["ladd24"],
                mod6=_hc["mod6"], eq9=_hc["eq9"], consts=_hc["consts"])
    in_maps = [dict(base, r3=np.ascontiguousarray(r3_l[c]))
               for c in range(NCORES)]
    res = run_bass_kernel_spmd(nc, in_maps, core_ids=list(range(NCORES)))
    return res.results


def rope_tables(core):
    q = core % 4
    half = 32
    freqs = (np.arange(half, dtype=np.float32) / np.float32(half)).astype(np.float32)
    theta = (1.0 / (10000.0 ** freqs.astype(np.float64))).astype(np.float32)
    pos = np.arange(q * TOK, (q + 1) * TOK, dtype=np.float32)
    ang = (pos[:, None] * theta[None, :]).astype(np.float32)
    cos = np.cos(ang.astype(np.float64)).astype(np.float32)
    sin = np.sin(ang.astype(np.float64)).astype(np.float32)
    return np.tile(cos, (1, 16)), np.tile(sin, (1, 16))


def run_l1a_qkv(r2_l, w_qkv, b_qkv):
    nc = build_l1a_qkv()
    base = dict(w_qkv=np.ascontiguousarray(w_qkv),
                b_qkv=b_qkv.reshape(1, 3 * D),
                ident_bf=to_bf16(_hc["ident_bf"]), ones_row=_hc["ones_row"],
                consts=_hc["consts"])
    in_maps = []
    for c in range(NCORES):
        cosr, sinr = rope_tables(c)
        in_maps.append(dict(base, r2=np.ascontiguousarray(r2_l[c]),
                            cosr=cosr, sinr=sinr))
    res = run_bass_kernel_spmd(nc, in_maps, core_ids=list(range(NCORES)))
    return res.results



def shard_tokens(a):
    out = []
    for c in range(NCORES):
        b, q = divmod(c, 4)
        out.append(np.ascontiguousarray(a[b, q * TOK:(q + 1) * TOK, :]))
    return out


def unshard_tokens(shards):
    a = np.empty((2, 8192, D), np.float32)
    for c, sh in enumerate(shards):
        b, q = divmod(c, 4)
        a[b, q * TOK:(q + 1) * TOK, :] = sh
    return a


_hc = host_consts()


def run_l0a_local(x_shards, Wk, bk, Wv, bv):
    nc = build_l0a_local()
    wkT = np.ascontiguousarray(Wk.T)
    wvT = np.ascontiguousarray(Wv.T)
    base = dict(wkT=wkT, wvT=wvT, bk=bk.reshape(1, D), bv=bv.reshape(1, D),
                ident_bf=to_bf16(_hc["ident_bf"]), ones_row=_hc["ones_row"],
                consts=_hc["consts"])
    in_maps = [dict(base, x=x_shards[c]) for c in range(NCORES)]
    res = run_bass_kernel_spmd(nc, in_maps, core_ids=list(range(NCORES)))
    return res.results


def kernel(**inputs):
    g = {k: np.ascontiguousarray(np.asarray(v, np.float32))
         for k, v in inputs.items()}
    x_shards = shard_tokens(g["L0_attn_x"])
    resA = run_l0a_local(x_shards, g["L0_attn_Wk"], g["L0_attn_bk"],
                         g["L0_attn_Wv"], g["L0_attn_bv"])
    kp_l = [np.asarray(r["kp"]) for r in resA]
    v_l = [np.asarray(r["v"]) for r in resA]
    sl_l = [np.asarray(r["s_local"]) for r in resA]
    ks_l = [np.asarray(r["kp_sum"]) for r in resA]
    resB = run_l0a_finish(x_shards, g["L0_attn_Wq"], g["L0_attn_bq"],
                          g["L0_attn_Wo"], g["L0_attn_bo"],
                          kp_l, v_l, sl_l, ks_l)
    r1_l = [np.ascontiguousarray(np.asarray(r["r1"])) for r in resB]
    resC = run_l0_ffn(r1_l, g["L0_ffn_scale"], g["L0_ffn_gate_w"],
                      g["L0_ffn_gate_b"], g["L0_ffn_mlp1_w"],
                      g["L0_ffn_mlp1_b"], g["L0_ffn_mlp2_w"],
                      g["L0_ffn_mlp2_b"])
    r2_l = [np.ascontiguousarray(np.asarray(r["r2"])) for r in resC]
    resD = run_l1a_qkv(r2_l, g["L1_attn_w_qkv"], g["L1_attn_b_qkv"])
    resE = run_l1a_attn(r2_l, resD, g["L1_attn_w_out"], g["L1_attn_b_out"])
    r3_l = [np.ascontiguousarray(np.asarray(r["r3"])) for r in resE]
    resF = run_l1_ffn(r3_l, g["L1_ffn_W1"], g["L1_ffn_b1"],
                      g["L1_ffn_W2"], g["L1_ffn_b2"])
    r4_l = [np.asarray(r["r4"]) for r in resF]
    return unshard_tokens(r4_l)
